# version 01
# baseline (speedup 1.0000x reference)
import contextlib
import math

import numpy as np
import ml_dtypes

import concourse.bass as bass
import concourse.mybir as mybir
from concourse.bass_utils import run_bass_kernel_spmd

F32 = mybir.dt.float32
BF16 = mybir.dt.bfloat16
I32 = mybir.dt.int32
AF = mybir.ActivationFunctionType
ALU = mybir.AluOpType
AX = mybir.AxisListType.X

B, S, D = 4, 2048, 4096
NH, NKV, HD = 32, 8, 128
LH, LKV = 16, 4
NCH = D // 128
TT = S // 128
OT = 8
QC = 4
MW = 2432
BIGM = 30000.0
SCALEQ = HD ** -0.5
RANKS = (8, 16, 32, 64)
EPS = 1e-5

_TAPS = ()
_BUILD_PHASES = 99
_SKIPS = ()


def _alibi_slopes(n=NH):
    cp2 = 2 ** math.floor(math.log2(n))
    base = 2.0 ** (-8.0 / cp2)
    slopes = [base ** i for i in range(1, cp2 + 1)]
    if cp2 != n:
        eb = 2.0 ** (-4.0 / cp2)
        slopes += [eb ** i for i in range(1, 1 + 2 * (n - cp2), 2)]
    return np.array(slopes, dtype=np.float32)


class _Sem:
    __slots__ = ("h", "n")

    def __init__(self, h):
        self.h = h
        self.n = 0


class _Plan:
    def __init__(self):
        self.ops = {k: [] for k in ("tensor", "vector", "scalar", "gpsimd", "sync")}

    def op(self, eng, fn, waits=(), inc=None, amt=1):
        tgt = None
        if inc is not None:
            inc.n += amt
            tgt = inc.n
        wl = tuple((s, v) for (s, v) in waits if v > 0)

        def run(e, _fn=fn, _w=wl, _s=inc, _a=amt):
            for s, v in _w:
                e.wait_ge(s.h, v)
            inst = _fn(e)
            if _s is not None:
                inst.then_inc(_s.h, _a)

        self.ops[eng].append(run)
        return tgt


def build(taps=(), build_phases=99, skips=()):
    nc = bass.Bass("TRN2", target_bir_lowering=False)
    P = _Plan()

    def ikind(name):
        return "ExternalOutput" if name in taps else "Internal"

    x_seq = nc.dram_tensor("x_seq", [S, D], F32, kind="ExternalInput").ap()
    x_own = nc.dram_tensor("x_own", [S // 2, D], F32, kind="ExternalInput").ap()
    wqk_t = nc.dram_tensor("wqk_t", [20, 128, 4096], F32, kind="ExternalInput").ap()
    wv_in = nc.dram_tensor("wv_in", [D, 512], F32, kind="ExternalInput").ap()
    wWt = nc.dram_tensor("wWt", [D, D], F32, kind="ExternalInput").ap()
    wop = nc.dram_tensor("wop", [D, D], F32, kind="ExternalInput").ap()
    at_in = nc.dram_tensor("at_in", [128, 2048], F32, kind="ExternalInput").ap()
    bbt = nc.dram_tensor("bbt", [65, D], F32, kind="ExternalInput").ap()
    gwt_in = nc.dram_tensor("gwt_in", [128, 128], F32, kind="ExternalInput").ap()
    gb_in = nc.dram_tensor("gb_in", [1, 4], F32, kind="ExternalInput").ap()
    nsl_in = nc.dram_tensor("nsl_in", [128, LH], F32, kind="ExternalInput").ap()
    prm = nc.dram_tensor("prm", [1, 8], I32, kind="ExternalInput").ap()
    y = nc.dram_tensor("y", [S // 2, D], F32, kind="ExternalOutput").ap()

    q_dram = nc.dram_tensor("q_dram", [LH, 128, S], BF16, kind=ikind("q_dram")).ap()
    xn_dram = nc.dram_tensor("xn_dram", [TT, 128, D], BF16, kind=ikind("xn_dram")).ap()
    ctx_a_dram = nc.dram_tensor("ctx_a_dram", [TT, 128, 2048], BF16,
                                kind=ikind("ctx_a_dram")).ap()
    ctx_b_dram = nc.dram_tensor("ctx_b_dram", [OT, 128, 2048], BF16,
                                kind=ikind("ctx_b_dram")).ap()
    lorat_dram = nc.dram_tensor("lorat_dram", [2, 65, 1024], BF16,
                                kind=ikind("lorat_dram")).ap()
    cc_in = nc.dram_tensor("cc_in", [2 * LH * 128, 1024], BF16, kind=ikind("cc_in")).ap()
    cc_out = nc.dram_tensor("cc_out", [16 * LH * 128, 1024], BF16,
                            kind=ikind("cc_out")).ap()

    tap_dram = {}
    for tname, shape, dt in (
            ("xn_fm", [128, NCH * S], BF16), ("k_sb", [128, LKV * S], BF16),
            ("v_sb", [128, LKV * 16 * 129], BF16), ("ctx_sb", [128, LH * S], BF16),
            ("oh_all", [128, 64], F32), ("negch", [128, LH], F32),
            ("lorat", [128, S], BF16), ("statw", [128, 64], F32)):
        if tname in taps:
            tap_dram[tname] = nc.dram_tensor("tap_" + tname, shape, dt,
                                             kind="ExternalOutput").ap()

    id_f32 = nc.inline_tensor(np.eye(128, dtype=np.float32), name="id_f32").ap()
    id_bf = nc.inline_tensor(np.eye(128, dtype=ml_dtypes.bfloat16), name="id_bf").ap()
    pp_np = np.zeros((4, 64), dtype=np.float32)
    for e_, r_ in enumerate(RANKS):
        pp_np[e_, :r_] = 1.0
    pp_c = nc.inline_tensor(pp_np, name="pp_c").ap()

    with contextlib.ExitStack() as st:
        AR = 100480
        arena = st.enter_context(nc.sbuf_tensor("arena", [128, AR], BF16))[:, :]

        def bf(off, n):
            assert off + n <= AR, (off, n)
            return arena[:, off:off + n]

        def f32v(off, n):
            assert off % 2 == 0 and off + 2 * n <= AR, (off, n)
            return arena[:, off:off + 2 * n].bitcast(F32)

        XNFM, CTX, MWO = 0, 0, 0
        MWW, QH = 32768, 32768
        TBUF = 36864
        PBUF = 40960
        KSQ = 41984
        DM = 44032
        PCTX = 49152
        XB0, WQB, MXN = 65536, 65536, 65536
        XB1, MCTX = 73728, 73728
        QST = 77824
        ATSB = 79872
        SCR, KSB, MBB = 81920, 81920, 81920
        MXO = 84992
        MOUT = 89088
        VSB = 90112
        WVB = 98368

        sm = st.enter_context(nc.sbuf_tensor("sm", [128, 704], F32))
        idf_sb = sm[:, 0:128]
        gwt_sb = sm[:, 128:256]
        negch_sb = sm[:, 256:256 + LH]
        nsl_sb = sm[:, 272:272 + LH]
        statw = sm[:, 288:352]
        oh_all = sm[:, 352:416]
        msb = sm[:, 416:480]
        sm2 = st.enter_context(nc.sbuf_tensor("sm2", [128, 256], BF16))
        idb_sb = sm2[:, 0:128]
        ones128_bf = sm2[:, 128:129]
        mlst = sm2[:, 130:194]
        sm3 = st.enter_context(nc.sbuf_tensor("sm3", [4, 256], F32))
        ohT_sb = sm3[0:4, 0:128]
        pp_sb = sm3[0:4, 128:192]
        sm4 = st.enter_context(nc.sbuf_tensor("sm4", [1, 192], F32))
        onesr_f = sm4[0:1, 0:128]
        gb_sb = sm4[0:1, 128:132]
        nrow = sm4[0:1, 132:192]
        prm_sb = st.enter_context(nc.sbuf_tensor("prm_sb", [1, 8], I32))[:, :]
        lorat_t = st.enter_context(nc.sbuf_tensor("lorat", [128, S + 1024], BF16))
        loraT = lorat_t[:, 0:S]
        loraT_own = lorat_t[:, S:S + 1024]

        pbank = [st.enter_context(nc.psum_tensor(f"pb{i}", [128, 512], F32))
                 for i in range(8)]

        def pb(i):
            return pbank[i][:, :]

        def pbbf(i):
            return pbank[i][:, :].bitcast(BF16)

        def sem(name):
            return _Sem(st.enter_context(nc.semaphore(name)))

        s_const = sem("s_const")
        s_x = sem("s_x")
        s_stat = sem("s_stat")
        s_scal = sem("s_scal")
        s_ln = sem("s_ln")
        s_tp = sem("s_tp")
        s_xn = sem("s_xn")
        s_sc = sem("s_sc")
        s_gpe = sem("s_gpe")
        s_gdve = sem("s_gdve")
        s_wq = sem("s_wq")
        s_qk = sem("s_qk")
        s_ev = sem("s_ev")
        s_vv = sem("s_vv")
        s_nm = sem("s_nm")
        s_lr = sem("s_lr")
        s_sp = sem("s_sp")
        s_qs = sem("s_qs")
        s_ql = sem("s_ql")
        s_sA = sem("s_sA")
        s_sT = sem("s_sT")
        s_sP = sem("s_sP")
        s_av = sem("s_av")
        s_cx = sem("s_cx")
        s_cc = sem("s_cc")
        s_mw = sem("s_mw")
        s_ml = sem("s_ml")
        s_mp = sem("s_mp")
        s_me = sem("s_me")
        s_out = sem("s_out")
        s_dm = sem("s_dm")
        s_sym = sem("s_sym")

        sym = {}

        P.op("sync", lambda e: e.dma_start(out=prm_sb, in_=prm[:, :]), inc=s_const, amt=16)
        c_prm = s_const.n
        P.op("sync", lambda e: e.dma_start(out=idf_sb, in_=id_f32), inc=s_const, amt=16)
        P.op("sync", lambda e: e.dma_start(out=idb_sb, in_=id_bf), inc=s_const, amt=16)
        P.op("sync", lambda e: e.dma_start(out=gwt_sb, in_=gwt_in[:, :]), inc=s_const, amt=16)
        P.op("sync", lambda e: e.dma_start(out=gb_sb, in_=gb_in[:, :]), inc=s_const, amt=16)
        P.op("sync", lambda e: e.dma_start(out=nsl_sb, in_=nsl_in[:, :]), inc=s_const, amt=16)
        P.op("sync", lambda e: e.dma_start(out=pp_sb, in_=pp_c), inc=s_const, amt=16)
        P.op("vector", lambda e: e.memset(ones128_bf, 1.0), inc=s_const)
        P.op("vector", lambda e: e.memset(onesr_f, 1.0), inc=s_const)
        P.op("vector", lambda e: e.memset(loraT[64:65, :], 1.0), inc=s_const)
        P.op("vector", lambda e: e.memset(bf(VSB, 8256), 1.0), inc=s_const)
        c_const = s_const.n

        def g_params(e):
            sym["own_tile0"] = e.value_load(prm_sb[0:1, 0:1])
            sym["prank"] = e.value_load(prm_sb[0:1, 2:3])
            sym["own_half"] = e.value_load(prm_sb[0:1, 3:4])
            return e.nop()

        P.op("sync", g_params, waits=[(s_const, c_prm)], inc=s_sym)

        xnfm = bf(XNFM, NCH * S)
        xn3 = xnfm.rearrange("p (c s) -> p c s", s=S)

        def xn_sl(ch, c0, n):
            return xnfm[:, ch * S + c0:ch * S + c0 + n]

        xb = [f32v(XB0, 4096), f32v(XB1, 4096)]
        scr = f32v(SCR, 4096)
        ln_done = [0] * TT
        xn_done = [0] * TT
        sc_done = [0] * TT
        tp_done = [0] * TT
        gpe_done = [0] * TT
        gd_done = [0] * TT
        xn_evac = []
        sc_evac = []

        for t in range(TT):
            bi = t % 2
            po = 16 * bi
            s_sum = statw[:, po + 0:po + 1]
            s_negm = statw[:, po + 1:po + 2]
            s_msq = statw[:, po + 2:po + 3]
            s_m2 = statw[:, po + 3:po + 4]
            s_var = statw[:, po + 4:po + 5]
            s_lnv = statw[:, po + 5:po + 6]
            s_rstd = statw[:, po + 6:po + 7]
            s_gm = statw[:, po + 7:po + 8]
            s_vt = statw[:, po + 10:po + 11]
            s_cum = statw[:, po + 8:po + 9]
            s_ncum = statw[:, po + 9:po + 10]

            w = [(s_tp, tp_done[t - 2])] if t >= 2 else []
            P.op("sync", lambda e, bi=bi, t=t: e.dma_start(
                out=xb[bi], in_=x_seq[t * 128:(t + 1) * 128, :]), waits=w, inc=s_x, amt=16)
            cx = s_x.n

            P.op("vector", lambda e, bi=bi, s_sum=s_sum: e.reduce_sum(s_sum, xb[bi], axis=AX),
                 waits=[(s_x, cx)], inc=s_scal)
            c1 = s_scal.n
            P.op("vector", lambda e, s_sum=s_sum, s_negm=s_negm: e.tensor_scalar(
                out=s_negm, in0=s_sum, scalar1=-1.0 / D, scalar2=None, op0=ALU.mult),
                 waits=[(s_scal, c1)], inc=s_scal)
            c_negm = s_scal.n
            w = [(s_x, cx)]
            if t >= 1:
                w.append((s_gpe, gpe_done[t - 1]))
            P.op("scalar", lambda e, bi=bi, s_msq=s_msq: e.activation(
                out=scr, in_=xb[bi], func=AF.Square, accum_out=s_msq), waits=w, inc=s_stat)
            c_msq = s_stat.n
            P.op("vector", lambda e, s_negm=s_negm, s_m2=s_m2: e.tensor_mul(s_m2, s_negm, s_negm),
                 waits=[(s_scal, c_negm)], inc=s_scal)
            c3 = s_scal.n
            P.op("vector", lambda e, s_msq=s_msq, s_vt=s_vt: e.tensor_scalar(
                out=s_vt, in0=s_msq, scalar1=1.0 / D, scalar2=EPS,
                op0=ALU.mult, op1=ALU.add),
                 waits=[(s_stat, c_msq)], inc=s_scal)
            c4 = s_scal.n
            P.op("vector", lambda e, s_vt=s_vt, s_m2=s_m2, s_var=s_var: e.tensor_sub(
                s_var, s_vt, s_m2), waits=[(s_scal, c4), (s_scal, c3)], inc=s_stat)
            c_var = s_stat.n
            P.op("scalar", lambda e, s_var=s_var, s_lnv=s_lnv: e.activation(
                out=s_lnv, in_=s_var, func=AF.Ln), waits=[(s_stat, c_var)])
            P.op("scalar", lambda e, s_lnv=s_lnv, s_rstd=s_rstd: e.activation(
                out=s_rstd, in_=s_lnv, func=AF.Exp, scale=-0.5), inc=s_stat)
            c_rstd = s_stat.n
            P.op("vector", lambda e, bi=bi, s_negm=s_negm, s_rstd=s_rstd: e.tensor_scalar(
                out=xb[bi], in0=xb[bi], scalar1=s_negm, scalar2=s_rstd,
                op0=ALU.add, op1=ALU.mult),
                 waits=[(s_scal, c_negm), (s_stat, c_rstd)], inc=s_ln)
            ln_done[t] = s_ln.n

            for g in range(8):
                bank = g % 2
                gg = t * 8 + g
                w = [(s_ln, ln_done[t]), (s_const, c_const)]
                if gg >= 2:
                    w += [(s_xn, xn_evac[gg - 2]), (s_sc, sc_evac[gg - 2])]
                for j in range(4):
                    ch = 4 * g + j
                    P.op("tensor", lambda e, bi=bi, ch=ch, j=j, bank=bank: e.transpose(
                        pb(bank)[:, j * 128:(j + 1) * 128],
                        xb[bi][:, ch * 128:(ch + 1) * 128], idf_sb),
                         waits=(w if j == 0 else []), inc=(s_tp if j == 3 else None))
                ctp = s_tp.n
                if g == 7:
                    tp_done[t] = ctp
                P.op("vector", lambda e, bank=bank, g=g, t=t: e.tensor_copy(
                    xn3[:, 4 * g:4 * g + 4, t * 128:(t + 1) * 128],
                    pb(bank).rearrange("p (c s) -> p c s", s=128)),
                     waits=[(s_tp, ctp)], inc=s_xn)
                xn_evac.append(s_xn.n)
                P.op("scalar", lambda e, bank=bank, g=g: e.copy(
                    out=scr[:, g * 512:(g + 1) * 512], in_=pb(bank)),
                     waits=[(s_xn, s_xn.n)], inc=s_sc)
                sc_evac.append(s_sc.n)
            xn_done[t] = s_xn.n
            sc_done[t] = s_sc.n

            gbank = 6 + (t % 2)
            w0 = [(s_sc, sc_done[t]), (s_const, c_const)]
            if t >= 2:
                w0.append((s_gdve, gd_done[t - 2]))
            for ch in range(NCH):
                P.op("tensor", lambda e, ch=ch, gbank=gbank: e.matmul(
                    pb(gbank)[:, 0:4], scr[:, ch * 128:(ch + 1) * 128],
                    gwt_sb[:, ch * 4:ch * 4 + 4],
                    start=(ch == 0), stop=False, skip_group_check=True),
                     waits=(w0 if ch == 0 else []))
            P.op("tensor", lambda e, gbank=gbank: e.matmul(
                pb(gbank)[:, 0:4], onesr_f, gb_sb,
                start=False, stop=True, skip_group_check=True), inc=s_gpe)
            gpe_done[t] = s_gpe.n

            ohc = oh_all[:, 4 * t:4 * t + 4]
            P.op("vector", lambda e, gbank=gbank, s_gm=s_gm: e.reduce_max(
                s_gm, pb(gbank)[:, 0:4], axis=AX),
                 waits=[(s_gpe, gpe_done[t])], inc=s_scal)
            c_gm = s_scal.n
            P.op("vector", lambda e, gbank=gbank, ohc=ohc, s_gm=s_gm: e.tensor_scalar(
                out=ohc, in0=pb(gbank)[:, 0:4], scalar1=s_gm, scalar2=None, op0=ALU.is_equal),
                 waits=[(s_scal, c_gm)], inc=s_scal)
            cch = s_scal.n
            P.op("vector", lambda e, ohc=ohc, s_cum=s_cum: e.tensor_copy(s_cum, ohc[:, 0:1]),
                 waits=[(s_scal, cch)], inc=s_scal)
            cch = s_scal.n
            for ei in range(1, 4):
                P.op("vector", lambda e, s_cum=s_cum, s_ncum=s_ncum: e.tensor_scalar(
                    out=s_ncum, in0=s_cum, scalar1=-1.0, scalar2=1.0,
                    op0=ALU.mult, op1=ALU.add),
                     waits=[(s_scal, cch)], inc=s_scal)
                cch = s_scal.n
                inc_ = s_gdve if ei == 3 else s_scal
                P.op("vector", lambda e, ei=ei, ohc=ohc, s_ncum=s_ncum: e.tensor_mul(
                    ohc[:, ei:ei + 1], ohc[:, ei:ei + 1], s_ncum),
                     waits=[(s_scal, cch)], inc=inc_)
                if ei < 3:
                    cch = s_scal.n
                    P.op("vector", lambda e, ei=ei, ohc=ohc, s_cum=s_cum: e.tensor_max(
                        s_cum, s_cum, ohc[:, ei:ei + 1]),
                         waits=[(s_scal, cch)], inc=s_scal)
                    cch = s_scal.n
            gd_done[t] = s_gdve.n

        if "statw" in tap_dram:
            P.op("sync", lambda e: e.dma_start(out=tap_dram["statw"], in_=statw),
                 waits=[(s_ln, ln_done[1])], inc=s_sp, amt=16)
        if "oh_all" in tap_dram:
            P.op("sync", lambda e: e.dma_start(out=tap_dram["oh_all"], in_=oh_all),
                 waits=[(s_gdve, s_gdve.n)], inc=s_sp, amt=16)
        if "xn_fm" in tap_dram and build_phases < 2:
            P.op("sync", lambda e: e.dma_start(out=tap_dram["xn_fm"], in_=xnfm),
                 waits=[(s_xn, s_xn.n)], inc=s_sp, amt=16)

        if build_phases >= 2:
            wv_full = bf(WQB, 16384)
            vsb = bf(VSB, 8256)
            skipset = set(skips)
            for gi in (range(8) if "V" not in skipset else ()):
                w = [(s_tp, tp_done[TT - 1]), (s_x, s_x.n)] if gi == 0 else []
                P.op("gpsimd", lambda e, gi=gi: e.dma_start(
                    out=wv_full[:, gi * 2048:(gi + 1) * 2048].rearrange(
                        "p (c s) -> p c s", s=512),
                    in_=wv_in.rearrange("(c p) s -> p c s", p=128)[:, 4 * gi:4 * gi + 4, :]),
                     waits=w, inc=s_wq, amt=16)
            c_wv = s_wq.n
            vv_done = [0] * TT
            for t in (range(TT) if "V" not in skipset else ()):
                bank = 4 + (t % 2)
                w = [(s_wq, c_wv), (s_xn, xn_done[t]), (s_const, c_const)]
                if t >= 2:
                    w.append((s_vv, vv_done[t - 2]))
                for ch in range(NCH):
                    P.op("tensor", lambda e, ch=ch, t=t, bank=bank: e.matmul(
                        pb(bank), xn_sl(ch, t * 128, 128),
                        wv_full[:, ch * 512:(ch + 1) * 512],
                        start=(ch == 0), stop=(ch == NCH - 1), skip_group_check=True),
                         waits=(w if ch == 0 else []),
                         inc=(s_vv if ch == NCH - 1 else None))
                cvp = s_vv.n
                P.op("vector", lambda e, bank=bank, t=t: e.tensor_copy(
                    vsb[:, t * (LKV * 129):(t + 1) * (LKV * 129)].rearrange(
                        "p (kv c) -> p kv c", c=129)[:, :, 0:128],
                    pb(bank).rearrange("p (kv c) -> p kv c", c=128)),
                     waits=[(s_vv, cvp)], inc=s_vv)
                vv_done[t] = s_vv.n
            c_vall = s_vv.n

            at_sb = bf(ATSB, 2048)
            P.op("gpsimd", lambda e: e.dma_start(out=at_sb, in_=at_in[:, :]),
                 waits=[(s_vv, c_vall)], inc=s_wq, amt=16)
            c_at = s_wq.n
            lr_done = [0] * TT
            for t in (range(TT) if "LORA" not in skipset else ()):
                w = [(s_wq, c_at), (s_xn, xn_done[t]), (s_gdve, gd_done[t]),
                     (s_const, c_const)]
                if t >= 1:
                    w.append((s_lr, lr_done[t - 1]))
                for ch in range(NCH):
                    P.op("tensor", lambda e, ch=ch, t=t: e.matmul(
                        pb(0)[:, 0:64], xn_sl(ch, t * 128, 128),
                        at_sb[:, ch * 64:(ch + 1) * 64],
                        start=(ch == 0), stop=(ch == NCH - 1), skip_group_check=True),
                         waits=(w if ch == 0 else []),
                         inc=(s_lr if ch == NCH - 1 else None))
                c_l1 = s_lr.n
                P.op("tensor", lambda e, t=t: e.transpose(
                    pb(1)[0:4, 0:128], oh_all[:, 4 * t:4 * t + 4], idf_sb), inc=s_lr)
                c_ot = s_lr.n
                P.op("scalar", lambda e: e.copy(out=ohT_sb, in_=pb(1)[0:4, 0:128]),
                     waits=[(s_lr, c_ot)], inc=s_lr)
                c_ot2 = s_lr.n
                P.op("tensor", lambda e: e.matmul(
                    pb(1)[:, 128:192], ohT_sb, pp_sb,
                    start=True, stop=True, skip_group_check=True),
                     waits=[(s_lr, c_ot2)], inc=s_lr)
                c_m = s_lr.n
                P.op("scalar", lambda e: e.copy(out=msb, in_=pb(1)[:, 128:192]),
                     waits=[(s_lr, c_m)], inc=s_lr)
                c_msb = s_lr.n
                P.op("vector", lambda e: e.tensor_mul(mlst, pb(0)[:, 0:64], msb),
                     waits=[(s_lr, c_msb), (s_lr, c_l1)], inc=s_lr)
                c_ml = s_lr.n
                P.op("tensor", lambda e: e.transpose(pbbf(1)[0:64, 0:128], mlst, idb_sb),
                     waits=[(s_lr, c_ml)], inc=s_lr)
                c_mlt = s_lr.n
                P.op("vector", lambda e, t=t: e.tensor_copy(
                    loraT[0:64, t * 128:(t + 1) * 128], pbbf(1)[0:64, 0:128]),
                     waits=[(s_lr, c_mlt)], inc=s_lr)
                lr_done[t] = s_lr.n
            c_lora = s_lr.n
            if "lorat" in tap_dram:
                P.op("sync", lambda e: e.dma_start(out=tap_dram["lorat"], in_=lorat_t[:, 0:S]),
                     waits=[(s_lr, c_lora)], inc=s_sp, amt=16)

            wqb = [bf(WQB, 4096), bf(WQB + 4096, 4096)]
            qst = bf(QST, 2048)
            ksb = bf(KSB, LKV * S)
            qsqs = bf(WVB, 2048)
            qk_done = [0] * 20
            ev_done = [0] * 20
            qs_done = [0] * 20
            nm_done = [0] * 20
            for ct in (range(20) if "QK" not in skipset else ()):
                bset = 0 if ct % 2 == 0 else 4
                bi = ct % 2
                w = []
                if ct < 2:
                    w.append((s_vv, c_vall))
                else:
                    w.append((s_qk, qk_done[ct - 2]))
                P.op("gpsimd", lambda e, ct=ct, bi=bi: e.dma_start(
                    out=wqb[bi], in_=wqk_t[ct, :, :]), waits=w, inc=s_wq, amt=16)
                c_w = s_wq.n
                w0 = [(s_wq, c_w), (s_xn, xn_done[TT - 1])]
                if ct == 0:
                    w0.append((s_lr, c_lora))
                if ct >= 2:
                    w0 += [(s_ev, ev_done[ct - 2]), (s_nm, nm_done[ct - 2])]
                for ch in range(NCH):
                    for tc in range(4):
                        P.op("tensor", lambda e, ch=ch, tc=tc, bi=bi, bset=bset: e.matmul(
                            pb(bset + tc), wqb[bi][:, ch * 128:(ch + 1) * 128],
                            xn_sl(ch, tc * 512, 512),
                            start=(ch == 0), stop=(ch == NCH - 1), skip_group_check=True),
                             waits=(w0 if (ch == 0 and tc == 0) else []),
                             inc=(s_qk if (ch == NCH - 1 and tc == 3) else None))
                qk_done[ct] = s_qk.n
                if ct < 16:
                    lh = ct
                    for tc in range(4):
                        w = [(s_qk, qk_done[ct])]
                        if ct >= 2:
                            w.append((s_qs, qs_done[ct - 2]))
                        P.op("scalar", lambda e, tc=tc, bset=bset: e.activation(
                            out=qst[:, tc * 512:(tc + 1) * 512], in_=pb(bset + tc),
                            func=AF.Copy, scale=SCALEQ),
                             waits=(w if tc == 0 else []))
                        P.op("scalar", lambda e, tc=tc, bset=bset: e.activation(
                            out=qsqs[:, tc * 512:(tc + 1) * 512], in_=pb(bset + tc),
                            func=AF.Square),
                             inc=(s_ev if tc == 3 else None))
                    ev_done[ct] = s_ev.n
                    P.op("sync", lambda e, lh=lh: e.dma_start(
                        out=q_dram[lh, :, :], in_=qst),
                         waits=[(s_ev, ev_done[ct])], inc=s_qs, amt=16)
                    qs_done[ct] = s_qs.n
                    for tc in range(4):
                        P.op("tensor", lambda e, tc=tc, bset=bset: e.matmul(
                            pb(bset + tc)[0:1, 0:512], ones128_bf,
                            qsqs[:, tc * 512:(tc + 1) * 512],
                            start=True, stop=True, skip_group_check=True),
                             waits=([(s_ev, ev_done[ct])] if tc == 0 else []),
                             inc=(s_nm if tc == 3 else None))
                    c_nmm = s_nm.n
                    for tc in range(4):
                        P.op("vector", lambda e, tc=tc, bset=bset: e.reduce_max(
                            nrow[0:1, 56 + tc:57 + tc], pb(bset + tc)[0:1, 0:512], axis=AX),
                             waits=([(s_nm, c_nmm)] if tc == 0 else []))
                    P.op("vector", lambda e, lh=lh: e.reduce_max(
                        nrow[0:1, lh:lh + 1], nrow[0:1, 56:60], axis=AX), inc=s_nm)
                    nm_done[ct] = s_nm.n
                else:
                    kv = ct - 16
                    for tc in range(4):
                        P.op("scalar", lambda e, tc=tc, kv=kv, bset=bset: e.activation(
                            out=ksb[:, kv * S + tc * 512:kv * S + (tc + 1) * 512],
                            in_=pb(bset + tc), func=AF.Copy),
                             waits=([(s_qk, qk_done[ct])] if tc == 0 else []),
                             inc=(s_ev if tc == 3 else None))
                    ev_done[ct] = s_ev.n
                    nm_done[ct] = s_nm.n
            c_kev = s_ev.n

            w_k0 = [(s_ev, c_kev), (s_nm, s_nm.n), (s_qk, qk_done[19])]
            kred_prev = 0
            kmm_prev = 0
            for kv in (range(LKV) if "NORM" not in skipset else ()):
                for half in range(2):
                    w_sq = list(w_k0) if (kv == 0 and half == 0) else []
                    if kv > 0 and half == 0:
                        w_sq.append((s_nm, kmm_prev))
                    P.op("scalar", lambda e, kv=kv, half=half: e.activation(
                        out=qsqs[:, half * 1024:(half + 1) * 1024],
                        in_=ksb[:, kv * S + half * 1024:kv * S + (half + 1) * 1024],
                        func=AF.Square),
                         waits=w_sq, inc=(s_nm if half == 1 else None))
                c_ks = s_nm.n
                if "NORMB" in skipset:
                    continue
                for tc in range(4):
                    P.op("tensor", lambda e, tc=tc: e.matmul(
                        pb(4 + tc)[0:1, 0:512], ones128_bf, qsqs[:, tc * 512:(tc + 1) * 512],
                        start=True, stop=True, skip_group_check=True),
                         waits=([(s_nm, c_ks), (s_qk, qk_done[19]), (s_ev, c_kev),
                                 (s_nm, kred_prev)] if tc == 0 else []),
                         inc=(s_nm if tc == 3 else None))
                kmm_prev = s_nm.n
                c_kmm = s_nm.n
                if "NORMC" in skipset:
                    continue
                cred = 0
                for tc in range(4):
                    P.op("vector", lambda e, tc=tc: e.reduce_max(
                        nrow[0:1, 56 + tc:57 + tc], pb(4 + tc)[0:1, 0:512], axis=AX),
                         waits=([(s_nm, c_kmm)] if tc == 0 else [(s_scal, cred)]),
                         inc=s_scal)
                    cred = s_scal.n
                P.op("vector", lambda e, kv=kv: e.reduce_max(
                    nrow[0:1, 16 + kv:17 + kv], nrow[0:1, 56:60], axis=AX),
                     waits=[(s_scal, cred)], inc=s_nm)
                kred_prev = s_nm.n
            c_kn = s_nm.n
            for kv in (range(LKV) if ("NORM" not in skipset and "NORMD" not in skipset
                                       and "NORMC" not in skipset and "NORMB" not in skipset)
                       else ()):
                P.op("vector", lambda e, kv=kv: e.tensor_scalar(
                    out=nrow[0:1, 24 + 4 * kv:28 + 4 * kv],
                    in0=nrow[0:1, 4 * kv:4 * kv + 4],
                    scalar1=nrow[0:1, 16 + kv:17 + kv], scalar2=None, op0=ALU.mult),
                     waits=[(s_nm, c_kn)] if kv == 0 else [], inc=(s_nm if kv == 3 else None))
            c_prod = s_nm.n
            if {"NORM", "NORMB", "NORMC", "NORMD", "NORME"} & skipset:
                c_negch = s_nm.n
            if not ({"NORM", "NORMB", "NORMC", "NORMD", "NORME"} & skipset):
                P.op("scalar", lambda e: e.activation(out=nrow[0:1, 40:56],
                                                      in_=nrow[0:1, 24:40], func=AF.Sqrt),
                     waits=[(s_nm, c_prod)])
                P.op("scalar", lambda e: e.activation(out=nrow[0:1, 24:40],
                                                      in_=nrow[0:1, 40:56], func=AF.Copy,
                                                      scale=-SCALEQ), inc=s_nm)
                c_negc = s_nm.n
                P.op("tensor", lambda e: e.matmul(pb(5)[:, 0:16], onesr_f, nrow[0:1, 24:40],
                                                  start=True, stop=True, skip_group_check=True),
                     waits=[(s_nm, c_negc), (s_vv, c_vall), (s_ev, c_kev)], inc=s_nm)
                c_bc = s_nm.n
                P.op("vector", lambda e: e.tensor_copy(negch_sb, pb(5)[:, 0:16]),
                     waits=[(s_nm, c_bc)], inc=s_nm)
                c_negch = s_nm.n

            for t in (range(TT) if "SPILL" not in skipset else ()):
                P.op("sync", lambda e, t=t: e.dma_start(
                    out=xn_dram[t, :, :].rearrange("p (c s) -> p c s", s=128),
                    in_=xn3[:, :, t * 128:(t + 1) * 128]),
                     waits=[(s_xn, xn_done[t])], inc=s_sp, amt=16)
            c_xnsp = s_sp.n
            if "SPILL" not in skipset:
                P.op("sync", lambda e: e.dma_start(
                    out=lorat_dram.rearrange("h r c -> r h c"),
                    in_=lorat_t[0:65, 0:S].rearrange("r (h c) -> r h c", c=1024)),
                     waits=[(s_lr, c_lora)], inc=s_sp, amt=16)
            c_lrsp = s_sp.n

            if "xn_fm" in tap_dram:
                P.op("sync", lambda e: e.dma_start(out=tap_dram["xn_fm"], in_=xnfm),
                     waits=[(s_qk, s_qk.n), (s_lr, c_lora)], inc=s_sp, amt=16)
            if "k_sb" in tap_dram:
                P.op("sync", lambda e: e.dma_start(out=tap_dram["k_sb"], in_=ksb),
                     waits=[(s_ev, s_ev.n)], inc=s_sp, amt=16)
            if "v_sb" in tap_dram:
                P.op("sync", lambda e: e.dma_start(out=tap_dram["v_sb"], in_=vsb),
                     waits=[(s_vv, c_vall)], inc=s_sp, amt=16)
            if "negch" in tap_dram:
                P.op("sync", lambda e: e.dma_start(out=tap_dram["negch"], in_=negch_sb),
                     waits=[(s_nm, c_negch)], inc=s_sp, amt=16)
            if "q_dram" in tap_dram or True:
                pass

        if build_phases >= 3:
            dm = f32v(DM, MW)
            dmneg = f32v(TBUF, MW)
            P.op("gpsimd", lambda e: e.iota(dm, [[1, MW]], base=-384, channel_multiplier=-1,
                                            allow_small_or_imprecise_dtypes=True),
                 waits=[(s_qk, qk_done[19]), (s_lr, c_lora), (s_sp, c_xnsp)], inc=s_dm)
            c_i = s_dm.n
            P.op("vector", lambda e: e.tensor_scalar(out=dmneg, in0=dm, scalar1=0.0,
                                                     scalar2=None, op0=ALU.is_lt),
                 waits=[(s_dm, c_i)], inc=s_dm)
            P.op("vector", lambda e: e.tensor_scalar(out=dm, in0=dm, scalar1=0.0,
                                                     scalar2=None, op0=ALU.max), inc=s_dm)
            c_relu = s_dm.n
            P.op("scalar", lambda e: e.activation(out=dm, in_=dm, func=AF.Ln),
                 waits=[(s_dm, c_relu)])
            P.op("scalar", lambda e: e.activation(out=dm, in_=dm, func=AF.Exp, scale=0.5),
                 inc=s_dm)
            c_sq = s_dm.n
            P.op("vector", lambda e: e.scalar_tensor_tensor(
                out=dm, in0=dmneg, scalar=BIGM, in1=dm, op0=ALU.mult, op1=ALU.add),
                 waits=[(s_dm, c_sq)], inc=s_dm)
            c_dm = s_dm.n

            ctxs = bf(CTX, LH * S)
            ctx3 = ctxs.rearrange("p (h s) -> p h s", s=S)
            qh = [bf(QH, 2048), bf(QH + 2048, 2048)]
            tb = [f32v(TBUF, 512), f32v(TBUF + 1024, 512)]
            pbuf = [bf(PBUF, 512), bf(PBUF + 512, 512)]
            stg = bf(KSQ, 2048)
            ql_done = [0] * LH
            qfree = [0, 0]
            sbank_free = [0, 0]
            tb_free = [0, 0]
            pb_free = [0, 0]
            att_ctx_free = 0
            ctx_ev = 0
            it = 0
            for lh in range(LH):
                bi = lh % 2
                kv = lh // 4
                w = [(s_qs, qs_done[lh]), (s_sp, c_lrsp),
                     (s_qk, qk_done[19]), (s_lr, c_lora)]
                if lh >= 2:
                    w.append((s_sA, qfree[bi]))
                P.op("sync", lambda e, lh=lh, bi=bi: e.dma_start(
                    out=qh[bi], in_=q_dram[lh, :, :]), waits=w, inc=s_ql, amt=16)
                ql_done[lh] = s_ql.n
                last_sA = 0
                for qc in range(QC):
                    nkb = 4 * (qc + 1)
                    for kb in range(nkb):
                        si = it % 2
                        w = [(s_ev, c_kev), (s_ql, ql_done[lh])]
                        if it >= 2:
                            w.append((s_sT, sbank_free[si]))
                        if kb == 0:
                            w.append((s_cx, att_ctx_free))
                            if it == 0:
                                w += [(s_nm, c_kn), (s_lr, c_lora)]
                        P.op("tensor", lambda e, kv=kv, kb=kb, qc=qc, si=si, bi=bi: e.matmul(
                            pb(si), ksb[:, kv * S + kb * 128:kv * S + kb * 128 + 128],
                            qh[bi][:, qc * 512:(qc + 1) * 512],
                            start=True, stop=True, skip_group_check=True),
                             waits=w, inc=s_sA)
                        c_s = s_sA.n
                        if kb == nkb - 1 and qc == QC - 1:
                            last_sA = c_s
                        off = 384 + qc * 512 - kb * 128
                        w = [(s_sA, c_s), (s_dm, c_dm), (s_const, c_const)]
                        if it >= 2:
                            w.append((s_sP, tb_free[si]))
                        P.op("vector", lambda e, off=off, si=si, lh=lh: e.scalar_tensor_tensor(
                            out=tb[si], in0=dm[:, off:off + 512],
                            scalar=nsl_sb[:, lh:lh + 1], in1=pb(si),
                            op0=ALU.mult, op1=ALU.add), waits=w, inc=s_sT)
                        c_t = s_sT.n
                        sbank_free[si] = c_t
                        w = [(s_sT, c_t), (s_nm, c_negch)]
                        if it >= 2:
                            w.append((s_av, pb_free[si]))
                        P.op("scalar", lambda e, si=si, lh=lh: e.activation(
                            out=pbuf[si], in_=tb[si], func=AF.Exp,
                            bias=negch_sb[:, lh:lh + 1]), waits=w, inc=s_sP)
                        c_p = s_sP.n
                        tb_free[si] = c_p
                        for sub in range(4):
                            P.op("tensor", lambda e, sub=sub, kb=kb, si=si, kv=kv, nkb=nkb: e.matmul(
                                pb(2 + sub)[:, 0:129], pbuf[si][:, sub * 128:(sub + 1) * 128],
                                bf(VSB, 8256)[:, kb * (LKV * 129) + kv * 129:
                                              kb * (LKV * 129) + kv * 129 + 129],
                                start=(kb == 0), stop=(kb == nkb - 1),
                                skip_group_check=True),
                                 waits=([(s_sP, c_p)] if sub == 0 else []),
                                 inc=(s_av if sub == 3 else None))
                        pb_free[si] = s_av.n
                        it += 1
                    c_avd = s_av.n
                    for sub in range(4):
                        rc = statw[:, 32 + sub:33 + sub]
                        P.op("vector", lambda e, sub=sub, rc=rc: e.reciprocal(
                            rc, pb(2 + sub)[:, 128:129]),
                             waits=([(s_av, c_avd)] if sub == 0 else []), inc=s_scal)
                        c_r = s_scal.n
                        P.op("vector", lambda e, sub=sub, rc=rc: e.tensor_scalar(
                            out=stg[:, sub * 128:(sub + 1) * 128], in0=pb(2 + sub)[:, 0:128],
                            scalar1=rc, scalar2=None, op0=ALU.mult),
                             waits=[(s_scal, c_r)], inc=s_cx)
                        c_n = s_cx.n
                        w = [(s_cx, c_n)]
                        if sub == 0:
                            w.append((s_cx, ctx_ev))
                        P.op("tensor", lambda e, sub=sub: e.transpose(
                            pbbf(6)[:, sub * 128:(sub + 1) * 128],
                            stg[:, sub * 128:(sub + 1) * 128], idb_sb),
                             waits=w, inc=(s_tp if sub == 3 else None))
                    att_ctx_free = s_cx.n
                    c_trp = s_tp.n
                    P.op("vector", lambda e, lh=lh, qc=qc: e.tensor_copy(
                        ctxs[:, lh * S + qc * 512:lh * S + (qc + 1) * 512], pbbf(6)[:, 0:512]),
                         waits=[(s_tp, c_trp)], inc=s_cx)
                    ctx_ev = s_cx.n
                qfree[bi] = last_sA
            c_att = s_cx.n
            if "ctx_sb" in tap_dram:
                P.op("sync", lambda e: e.dma_start(out=tap_dram["ctx_sb"], in_=ctxs),
                     waits=[(s_cx, c_att)], inc=s_sp, amt=16)

        if build_phases >= 4:
            for half in range(2):
                P.op("sync", lambda e, half=half: e.dma_start(
                    out=cc_in[half * 2048:(half + 1) * 2048, :].rearrange(
                        "(h p) c -> p h c", p=128),
                    in_=ctx3[:, :, half * 1024:(half + 1) * 1024]),
                     waits=[(s_cx, c_att)], inc=s_cc, amt=16)
            c_send = s_cc.n
            P.op("gpsimd", lambda e: e.collective_compute(
                "AllGather", ALU.bypass, replica_groups=[list(range(8))],
                ins=[cc_in], outs=[cc_out]),
                 waits=[(s_cc, c_send)], inc=s_cc)
            c_ag = s_cc.n
            pctx = bf(PCTX, LH * 1024)
            P.op("sync", lambda e: e.dma_start(
                out=pctx.rearrange("p (h c) -> p h c", c=1024),
                in_=cc_out[bass.ts(sym["prank"] * 2 + sym["own_half"], LH * 128),
                           :].rearrange("(h p) c -> p h c", p=128)),
                 waits=[(s_cc, c_ag), (s_sym, 1)], inc=s_cc, amt=16)
            c_recv = s_cc.n
            for t in range(TT):
                P.op("sync", lambda e, t=t: e.dma_start(
                    out=ctx_a_dram[t, :, :].rearrange("p (h c) -> p h c", c=128),
                    in_=ctx3[:, :, t * 128:(t + 1) * 128]),
                     waits=[(s_cx, c_att)], inc=s_cc, amt=16)
            for t in range(OT):
                P.op("sync", lambda e, t=t: e.dma_start(
                    out=ctx_b_dram[t, :, :].rearrange("p (h c) -> p h c", c=128),
                    in_=pctx.rearrange("p (h c) -> p h c", c=1024)[:, :, t * 128:(t + 1) * 128]),
                     waits=[(s_cc, c_recv)], inc=s_cc, amt=16)
            P.op("sync", lambda e: e.dma_start(
                out=loraT_own[0:65, :],
                in_=lorat_dram[bass.ts(sym["own_half"], 1), :, :].rearrange(
                    "a r c -> (a r) c")),
                 waits=[(s_sp, c_lrsp), (s_sym, 1)], inc=s_cc, amt=16)
            c_asm = s_cc.n

        if build_phases >= 5:
            wW = bf(MWW, 32768)
            wo = bf(MWO, 32768)
            bbb = [bf(MBB, 1024), bf(MBB + 1024, 1024)]
            xnl = [bf(MXN, 4096), bf(MXN + 4096, 4096)]
            ctxl = [bf(MCTX, 4096), bf(MCTX + 4096, 4096)]
            xsl = [f32v(MXO, 1024), f32v(MXO + 2048, 1024)]
            outs = [f32v(MOUT, 1024), f32v(MOUT + 2048, 1024)]
            mw_done = [0] * 4
            mq_done = [0] * 4
            mm_done = {}
            ev_done_m = {}
            od_done = {}
            for q in range(4):
                w = [(s_cc, c_asm)]
                if q >= 1:
                    w.append((s_mp, mq_done[q - 1]))
                for i in range(8):
                    P.op("gpsimd", lambda e, q=q, i=i: e.dma_start(
                        out=wW[:, i * 4096:(i + 1) * 4096].rearrange("p (c s) -> p c s", s=1024),
                        in_=wWt.rearrange("(c p) s -> p c s", p=128)[
                            :, 4 * i:4 * i + 4, q * 1024:(q + 1) * 1024]),
                         waits=(w if i == 0 else []), inc=s_mw, amt=16)
                for i in range(8):
                    P.op("gpsimd", lambda e, q=q, i=i: e.dma_start(
                        out=wo[:, i * 4096:(i + 1) * 4096].rearrange("p (c s) -> p c s", s=1024),
                        in_=wop.rearrange("(c p) s -> p c s", p=128)[
                            :, 4 * i:4 * i + 4, q * 1024:(q + 1) * 1024]),
                         inc=s_mw, amt=16)
                P.op("gpsimd", lambda e, q=q: e.dma_start(
                    out=bbb[q % 2][0:65, :], in_=bbt[:, q * 1024:(q + 1) * 1024]),
                     waits=([(s_sA, last_sA)] if q == 0 else []), inc=s_mw, amt=16)
                mw_done[q] = s_mw.n
                for t in range(OT):
                    gi = q * OT + t
                    bset = (t % 2) * 2
                    w = [(s_cc, c_asm)]
                    if gi >= 2:
                        w += [(s_mp, mm_done.get(gi - 2, 0)),
                              (s_me, ev_done_m.get(gi - 2, 0))]
                    P.op("sync", lambda e, t=t: e.dma_start(
                        out=xnl[t % 2],
                        in_=xn_dram[bass.ts(sym["own_tile0"] + t, 1), :, :].rearrange(
                            "a p c -> (a p) c")),
                         waits=w + [(s_sym, 1)], inc=s_ml, amt=16)
                    P.op("sync", lambda e, t=t: e.dma_start(
                        out=ctxl[t % 2][:, 0:2048],
                        in_=ctx_a_dram[bass.ts(sym["own_tile0"] + t, 1), :, :].rearrange(
                            "a p c -> (a p) c")),
                         inc=s_ml, amt=16)
                    P.op("sync", lambda e, t=t: e.dma_start(
                        out=ctxl[t % 2][:, 2048:4096], in_=ctx_b_dram[t, :, :]),
                         inc=s_ml, amt=16)
                    wx = [(s_out, od_done.get(gi - 2, 0))] if gi >= 2 else []
                    P.op("sync", lambda e, q=q, t=t: e.dma_start(
                        out=xsl[t % 2], in_=x_own[t * 128:(t + 1) * 128, q * 1024:(q + 1) * 1024]),
                         waits=wx, inc=s_ml, amt=16)
                    c_lh2 = s_ml.n
                    w0 = [(s_ml, c_lh2), (s_mw, mw_done[q])]
                    if gi >= 2:
                        w0.append((s_me, ev_done_m.get(gi - 2, 0)))
                    first = True
                    for ch in range(NCH):
                        for hf in range(2):
                            P.op("tensor", lambda e, ch=ch, hf=hf, t=t, bset=bset: e.matmul(
                                pb(bset + hf), xnl[t % 2][:, ch * 128:(ch + 1) * 128],
                                wW[:, ch * 1024 + hf * 512:ch * 1024 + (hf + 1) * 512],
                                start=(ch == 0), stop=False, skip_group_check=True),
                                 waits=(w0 if first else []))
                            first = False
                    for ch in range(NCH):
                        for hf in range(2):
                            P.op("tensor", lambda e, ch=ch, hf=hf, t=t, bset=bset: e.matmul(
                                pb(bset + hf), ctxl[t % 2][:, ch * 128:(ch + 1) * 128],
                                wo[:, ch * 1024 + hf * 512:ch * 1024 + (hf + 1) * 512],
                                start=False, stop=False, skip_group_check=True))
                    for hf in range(2):
                        P.op("tensor", lambda e, hf=hf, t=t, q=q, bset=bset: e.matmul(
                            pb(bset + hf), loraT_own[0:65, t * 128:(t + 1) * 128],
                            bbb[q % 2][0:65, hf * 512:(hf + 1) * 512],
                            start=False, stop=True, skip_group_check=True),
                             inc=(s_mp if hf == 1 else None))
                    mm_done[gi] = s_mp.n
                    if t == OT - 1:
                        mq_done[q] = s_mp.n
                    c_mm = s_mp.n
                    P.op("vector", lambda e, t=t, bset=bset: e.tensor_add(
                        outs[t % 2][:, 0:512], pb(bset), xsl[t % 2][:, 0:512]),
                         waits=[(s_mp, c_mm)])
                    P.op("vector", lambda e, t=t, bset=bset: e.tensor_add(
                        outs[t % 2][:, 512:1024], pb(bset + 1), xsl[t % 2][:, 512:1024]),
                         inc=s_me)
                    ev_done_m[gi] = s_me.n
                    P.op("sync", lambda e, q=q, t=t: e.dma_start(
                        out=y[t * 128:(t + 1) * 128, q * 1024:(q + 1) * 1024],
                        in_=outs[t % 2]),
                         waits=[(s_me, ev_done_m[gi])], inc=s_out, amt=16)
                    od_done[gi] = s_out.n
            P.op("sync", lambda e: e.nop(), waits=[(s_out, s_out.n)])

        with nc.Block() as block:
            @block.tensor
            def _(e):
                for fn in P.ops["tensor"]:
                    fn(e)

            @block.vector
            def _(e):
                for fn in P.ops["vector"]:
                    fn(e)

            @block.scalar
            def _(e):
                for fn in P.ops["scalar"]:
                    fn(e)

            @block.gpsimd
            def _(e):
                for fn in P.ops["gpsimd"]:
                    fn(e)

            @block.sync
            def _(e):
                for fn in P.ops["sync"]:
                    fn(e)

    return nc


def _prep_inputs(x, wqkv, wo, W, b, A, Bm, gw, gb):
    slopes = _alibi_slopes()
    wWt = np.ascontiguousarray(W.T)
    at_t = np.ascontiguousarray(
        A.T.reshape(32, 128, 64).transpose(1, 0, 2).reshape(128, 2048))
    bbt = np.ascontiguousarray(np.concatenate([Bm.T, b[None, :]], axis=0))
    gwt_t = np.ascontiguousarray(
        gw.T.reshape(32, 128, 4).transpose(1, 0, 2).reshape(128, 128))
    gb2 = np.ascontiguousarray(gb.reshape(1, 4))
    maps = []
    for c in range(8):
        bq, p = c // 2, c % 2
        qcols = wqkv[:, 2048 * p:2048 * p + 2048]
        kcols = wqkv[:, 4096 + 512 * p:4096 + 512 * p + 512]
        cols20 = np.concatenate([qcols, kcols], axis=1)
        wqk = np.ascontiguousarray(
            cols20.reshape(32, 128, 20, 128).transpose(2, 1, 0, 3).reshape(20, 128, 4096))
        wv = np.ascontiguousarray(wqkv[:, 5120 + 512 * p:5120 + 512 * p + 512])
        perm = np.r_[2048 * p:2048 * p + 2048, 2048 * (1 - p):2048 * (1 - p) + 2048]
        wopm = np.ascontiguousarray(wo[perm])
        nsl = np.ascontiguousarray(
            np.broadcast_to(-slopes[16 * p:16 * p + 16][None, :], (128, LH)))
        prm = np.array([[8 * p, 8 * (1 - p), c ^ 1, p, 1 - p, 0, 0, 0]], dtype=np.int32)
        maps.append({
            "x_seq": np.ascontiguousarray(x[bq]),
            "x_own": np.ascontiguousarray(x[bq, 1024 * p:1024 * p + 1024]),
            "wqk_t": wqk, "wv_in": wv, "wWt": wWt, "wop": wopm,
            "at_in": at_t, "bbt": bbt, "gwt_in": gwt_t, "gb_in": gb2,
            "nsl_in": nsl, "prm": prm,
        })
    return maps


_NC_CACHE = {}


def kernel(L0_attn_hidden_states, L0_attn_w_qkv, L0_attn_w_o, L0_ffn_W, L0_ffn_b,
           L0_ffn_A, L0_ffn_B, L0_ffn_gate_w, L0_ffn_gate_b):
    x = np.ascontiguousarray(np.asarray(L0_attn_hidden_states, dtype=np.float32))
    wqkv = np.asarray(L0_attn_w_qkv, np.float32)
    wo = np.asarray(L0_attn_w_o, np.float32)
    W = np.asarray(L0_ffn_W, np.float32)
    b = np.asarray(L0_ffn_b, np.float32)
    A = np.asarray(L0_ffn_A, np.float32)
    Bm = np.asarray(L0_ffn_B, np.float32)
    gw = np.asarray(L0_ffn_gate_w, np.float32)
    gb = np.asarray(L0_ffn_gate_b, np.float32)

    key = (tuple(_TAPS), _BUILD_PHASES, tuple(_SKIPS))
    if key not in _NC_CACHE:
        _NC_CACHE.clear()
        _NC_CACHE[key] = build(taps=_TAPS, build_phases=_BUILD_PHASES, skips=_SKIPS)
    nc = _NC_CACHE[key]

    in_maps = _prep_inputs(x, wqkv, wo, W, b, A, Bm, gw, gb)
    res = run_bass_kernel_spmd(nc, in_maps, core_ids=list(range(8)))
    out = np.empty((B, S, D), dtype=np.float32)
    for c in range(8):
        bq, p = c // 2, c % 2
        out[bq, 1024 * p:1024 * p + 1024] = res.results[c]["y"]
    kernel._last_results = res.results
    return out


# version 02 (final)
# speedup vs baseline: 1.5131x; 1.5131x over previous
import contextlib
import math

import numpy as np
import ml_dtypes

import concourse.bass as bass
import concourse.mybir as mybir
from concourse.bass_utils import run_bass_kernel_spmd

F32 = mybir.dt.float32
BF16 = mybir.dt.bfloat16
I32 = mybir.dt.int32
AF = mybir.ActivationFunctionType
ALU = mybir.AluOpType
AX = mybir.AxisListType.X

B, S, D = 4, 2048, 4096
NH, NKV, HD = 32, 8, 128
LH, LKV = 16, 4
NCH = D // 128
TT = S // 128
OT = 8
QC = 4
MW = 2432
BIGM = 30000.0
SCALEQ = HD ** -0.5
RANKS = (8, 16, 32, 64)
EPS = 1e-5

_TAPS = ()
_BUILD_PHASES = 99
_SKIPS = ()


def _alibi_slopes(n=NH):
    cp2 = 2 ** math.floor(math.log2(n))
    base = 2.0 ** (-8.0 / cp2)
    slopes = [base ** i for i in range(1, cp2 + 1)]
    if cp2 != n:
        eb = 2.0 ** (-4.0 / cp2)
        slopes += [eb ** i for i in range(1, 1 + 2 * (n - cp2), 2)]
    return np.array(slopes, dtype=np.float32)


class _Sem:
    __slots__ = ("h", "n")

    def __init__(self, h):
        self.h = h
        self.n = 0


class _Plan:
    def __init__(self):
        self.ops = {k: [] for k in ("tensor", "vector", "scalar", "gpsimd", "sync")}

    def op(self, eng, fn, waits=(), inc=None, amt=1):
        tgt = None
        if inc is not None:
            inc.n += amt
            tgt = inc.n
        wl = tuple((s, v) for (s, v) in waits if v > 0)

        def run(e, _fn=fn, _w=wl, _s=inc, _a=amt):
            for s, v in _w:
                e.wait_ge(s.h, v)
            inst = _fn(e)
            if _s is not None:
                inst.then_inc(_s.h, _a)

        self.ops[eng].append(run)
        return tgt


def build(taps=(), build_phases=99, skips=()):
    nc = bass.Bass("TRN2", target_bir_lowering=False)
    P = _Plan()

    def ikind(name):
        return "ExternalOutput" if name in taps else "Internal"

    x_seq = nc.dram_tensor("x_seq", [S, D], F32, kind="ExternalInput").ap()
    x_own = nc.dram_tensor("x_own", [S // 2, D], F32, kind="ExternalInput").ap()
    wqk_t = nc.dram_tensor("wqk_t", [20, 128, 4096], F32, kind="ExternalInput").ap()
    wv_in = nc.dram_tensor("wv_in", [D, 512], F32, kind="ExternalInput").ap()
    wWt = nc.dram_tensor("wWt", [D, D], F32, kind="ExternalInput").ap()
    wop = nc.dram_tensor("wop", [D, D], F32, kind="ExternalInput").ap()
    at_in = nc.dram_tensor("at_in", [128, 2048], F32, kind="ExternalInput").ap()
    bbt = nc.dram_tensor("bbt", [65, D], F32, kind="ExternalInput").ap()
    gwt_in = nc.dram_tensor("gwt_in", [128, 128], F32, kind="ExternalInput").ap()
    gb_in = nc.dram_tensor("gb_in", [1, 4], F32, kind="ExternalInput").ap()
    nsl_in = nc.dram_tensor("nsl_in", [128, LH], F32, kind="ExternalInput").ap()
    prm = nc.dram_tensor("prm", [1, 8], I32, kind="ExternalInput").ap()
    y = nc.dram_tensor("y", [S // 2, D], F32, kind="ExternalOutput").ap()

    q_dram = nc.dram_tensor("q_dram", [LH, 128, S], BF16, kind=ikind("q_dram")).ap()
    xnh_dram = nc.dram_tensor("xnh_dram", [2 * NCH, 128, 1024], BF16,
                              kind=ikind("xnh_dram")).ap()
    ctxh_dram = nc.dram_tensor("ctxh_dram", [2 * LH, 128, 1024], BF16,
                               kind=ikind("ctxh_dram")).ap()
    lorat_dram = nc.dram_tensor("lorat_dram", [2, 65, 1024], BF16,
                                kind=ikind("lorat_dram")).ap()
    ccw_in = nc.dram_tensor("ccw_in", [128, 64], BF16, kind="Internal").ap()
    ccw_out = nc.dram_tensor("ccw_out", [8 * 128, 64], BF16, kind="Internal").ap()
    cc_in = nc.dram_tensor("cc_in", [2 * LH * 128, 1024], BF16, kind=ikind("cc_in")).ap()
    cc_out = nc.dram_tensor("cc_out", [16, LH, 128, 1024], BF16,
                            kind=ikind("cc_out")).ap()

    tap_dram = {}
    for tname, shape, dt in (
            ("xn_fm", [128, NCH * S], BF16), ("k_sb", [128, LKV * S], BF16),
            ("v_sb", [128, LKV * 16 * 129], BF16), ("ctx_sb", [128, LH * S], BF16),
            ("oh_all", [128, 64], F32), ("negch", [128, LH], F32),
            ("lorat", [128, S], BF16), ("statw", [128, 64], F32),
            ("xnres", [128, 32768], BF16), ("ring0", [128, 8192], BF16),
            ("xnres2", [128, 32768], BF16)):
        if tname in taps:
            tap_dram[tname] = nc.dram_tensor("tap_" + tname, shape, dt,
                                             kind="ExternalOutput").ap()

    id_f32 = nc.inline_tensor(np.eye(128, dtype=np.float32), name="id_f32").ap()
    id_bf = nc.inline_tensor(np.eye(128, dtype=ml_dtypes.bfloat16), name="id_bf").ap()
    pp_np = np.zeros((4, 64), dtype=np.float32)
    for e_, r_ in enumerate(RANKS):
        pp_np[e_, :r_] = 1.0
    pp_c = nc.inline_tensor(pp_np, name="pp_c").ap()

    with contextlib.ExitStack() as st:
        AR = 100480
        arena = st.enter_context(nc.sbuf_tensor("arena", [128, AR], BF16))[:, :]

        def bf(off, n):
            assert off + n <= AR, (off, n)
            return arena[:, off:off + n]

        def f32v(off, n):
            assert off % 2 == 0 and off + 2 * n <= AR, (off, n)
            return arena[:, off:off + 2 * n].bitcast(F32)

        XNFM, CTX = 0, 0
        QH = 32768
        PBUF = 36864
        MBF = 38400
        DM = 40960
        STG = 45824
        SCID = 47872
        DMNEG = 49152
        XB0, WQB = 65536, 65536
        XB1 = 73728
        QST = 77824
        ATSB = 79872
        SCR, KSB = 81920, 81920
        VSB = 90112
        WVB = 98368
        XRES = 0
        RING = 32768
        MXS = 57344
        MOU = 59392
        MBB = 61440
        CTXRES = 66560

        sm = st.enter_context(nc.sbuf_tensor("sm", [128, 704], F32))
        idf_sb = sm[:, 0:128]
        gwt_sb = sm[:, 128:256]
        negch_sb = sm[:, 256:256 + LH]
        nsl_sb = sm[:, 272:272 + LH]
        statw = sm[:, 288:352]
        oh_all = sm[:, 352:416]
        msb = sm[:, 416:480]
        sm2 = st.enter_context(nc.sbuf_tensor("sm2", [128, 256], BF16))
        idb_sb = sm2[:, 0:128]
        ones128_bf = sm2[:, 128:129]
        mlst = sm2[:, 130:194]
        sm3 = st.enter_context(nc.sbuf_tensor("sm3", [4, 256], F32))
        ohT_sb = sm3[0:4, 0:128]
        pp_sb = sm3[0:4, 128:192]
        sm4 = st.enter_context(nc.sbuf_tensor("sm4", [1, 192], F32))
        onesr_f = sm4[0:1, 0:128]
        gb_sb = sm4[0:1, 128:132]
        nrow = sm4[0:1, 132:192]
        prm_sb = st.enter_context(nc.sbuf_tensor("prm_sb", [1, 8], I32))[:, :]
        lorat_t = st.enter_context(nc.sbuf_tensor("lorat", [128, S + 1024], BF16))
        loraT = lorat_t[:, 0:S]
        loraT_own = lorat_t[:, S:S + 1024]

        pbank = [st.enter_context(nc.psum_tensor(f"pb{i}", [128, 512], F32))
                 for i in range(8)]

        def pb(i):
            return pbank[i][:, :]

        def pbbf(i):
            return pbank[i][:, :].bitcast(BF16)

        def sem(name):
            return _Sem(st.enter_context(nc.semaphore(name)))

        s_const = sem("s_const")
        s_x = sem("s_x")
        s_stat = sem("s_stat")
        s_scal = sem("s_scal")
        s_ln = sem("s_ln")
        s_tp = sem("s_tp")
        s_xn = sem("s_xn")
        s_sc = sem("s_sc")
        s_gpe = sem("s_gpe")
        s_gdve = sem("s_gdve")
        s_wq = sem("s_wq")
        s_qk = sem("s_qk")
        s_ev = sem("s_ev")
        s_vv = sem("s_vv")
        s_nm = sem("s_nm")
        s_lr = sem("s_lr")
        s_sp = sem("s_sp")
        s_qs = sem("s_qs")
        s_ql = sem("s_ql")
        s_sA = sem("s_sA")
        s_sT = sem("s_sT")
        s_sP = sem("s_sP")
        s_av = sem("s_av")
        s_cx = sem("s_cx")
        s_cc = sem("s_cc")
        s_mw = sem("s_mw")
        s_ml = sem("s_ml")
        s_mp = sem("s_mp")
        s_me = sem("s_me")
        s_out = sem("s_out")
        s_dm = sem("s_dm")
        s_sym = sem("s_sym")

        sym = {}
        sym_sc = {}

        P.op("sync", lambda e: e.dma_start(out=prm_sb, in_=prm[:, :]), inc=s_const, amt=16)
        c_prm = s_const.n
        P.op("sync", lambda e: e.dma_start(out=idf_sb, in_=id_f32), inc=s_const, amt=16)
        P.op("sync", lambda e: e.dma_start(out=idb_sb, in_=id_bf), inc=s_const, amt=16)
        P.op("sync", lambda e: e.dma_start(out=gwt_sb, in_=gwt_in[:, :]), inc=s_const, amt=16)
        P.op("sync", lambda e: e.dma_start(out=gb_sb, in_=gb_in[:, :]), inc=s_const, amt=16)
        P.op("sync", lambda e: e.dma_start(out=nsl_sb, in_=nsl_in[:, :]), inc=s_const, amt=16)
        P.op("sync", lambda e: e.dma_start(out=pp_sb, in_=pp_c), inc=s_const, amt=16)
        P.op("vector", lambda e: e.memset(ones128_bf, 1.0), inc=s_const)
        P.op("vector", lambda e: e.memset(onesr_f, 1.0), inc=s_const)
        P.op("vector", lambda e: e.memset(loraT[64:65, :], 1.0), inc=s_const)
        P.op("vector", lambda e: e.memset(bf(VSB, 8256), 1.0), inc=s_const)
        c_const = s_const.n

        def g_params(e):
            sym["own_tile0"] = e.value_load(prm_sb[0:1, 0:1])
            sym["prank"] = e.value_load(prm_sb[0:1, 2:3])
            sym["own_half"] = e.value_load(prm_sb[0:1, 3:4])
            return e.nop()

        P.op("sync", g_params, waits=[(s_const, c_prm)], inc=s_sym)

        def g_params_sc(e):
            sym_sc["own_half"] = e.value_load(prm_sb[0:1, 3:4])
            sym_sc["pblk"] = e.value_load(prm_sb[0:1, 5:6])
            return e.nop()

        P.op("scalar", g_params_sc, waits=[(s_const, c_prm)], inc=s_sym)

        xnfm = bf(XNFM, NCH * S)
        xn3 = xnfm.rearrange("p (c s) -> p c s", s=S)

        def xn_sl(ch, c0, n):
            return xnfm[:, ch * S + c0:ch * S + c0 + n]

        xb = [f32v(XB0, 4096), f32v(XB1, 4096)]
        scr = f32v(SCR, 4096)
        ln_done = [0] * TT
        xn_done = [0] * TT
        sc_done = [0] * TT
        tp_done = [0] * TT
        gpe_done = [0] * TT
        gd_done = [0] * TT
        xn_evac = []
        sc_evac = []

        for t in range(TT):
            bi = t % 2
            po = 16 * bi
            s_sum = statw[:, po + 0:po + 1]
            s_negm = statw[:, po + 1:po + 2]
            s_msq = statw[:, po + 2:po + 3]
            s_m2 = statw[:, po + 3:po + 4]
            s_var = statw[:, po + 4:po + 5]
            s_lnv = statw[:, po + 5:po + 6]
            s_rstd = statw[:, po + 6:po + 7]
            s_gm = statw[:, po + 7:po + 8]
            s_vt = statw[:, po + 10:po + 11]
            s_cum = statw[:, po + 8:po + 9]
            s_ncum = statw[:, po + 9:po + 10]

            w = [(s_tp, tp_done[t - 2])] if t >= 2 else []
            P.op("sync", lambda e, bi=bi, t=t: e.dma_start(
                out=xb[bi], in_=x_seq[t * 128:(t + 1) * 128, :]), waits=w, inc=s_x, amt=16)
            cx = s_x.n

            P.op("vector", lambda e, bi=bi, s_sum=s_sum: e.reduce_sum(s_sum, xb[bi], axis=AX),
                 waits=[(s_x, cx)], inc=s_scal)
            c1 = s_scal.n
            P.op("vector", lambda e, s_sum=s_sum, s_negm=s_negm: e.tensor_scalar(
                out=s_negm, in0=s_sum, scalar1=-1.0 / D, scalar2=None, op0=ALU.mult),
                 waits=[(s_scal, c1)], inc=s_scal)
            c_negm = s_scal.n
            w = [(s_x, cx)]
            if t >= 1:
                w.append((s_gpe, gpe_done[t - 1]))
            P.op("scalar", lambda e, bi=bi, s_msq=s_msq: e.activation(
                out=scr, in_=xb[bi], func=AF.Square, accum_out=s_msq), waits=w, inc=s_stat)
            c_msq = s_stat.n
            P.op("vector", lambda e, s_negm=s_negm, s_m2=s_m2: e.tensor_mul(s_m2, s_negm, s_negm),
                 waits=[(s_scal, c_negm)], inc=s_scal)
            c3 = s_scal.n
            P.op("vector", lambda e, s_msq=s_msq, s_vt=s_vt: e.tensor_scalar(
                out=s_vt, in0=s_msq, scalar1=1.0 / D, scalar2=EPS,
                op0=ALU.mult, op1=ALU.add),
                 waits=[(s_stat, c_msq)], inc=s_scal)
            c4 = s_scal.n
            P.op("vector", lambda e, s_vt=s_vt, s_m2=s_m2, s_var=s_var: e.tensor_sub(
                s_var, s_vt, s_m2), waits=[(s_scal, c4), (s_scal, c3)], inc=s_stat)
            c_var = s_stat.n
            P.op("scalar", lambda e, s_var=s_var, s_lnv=s_lnv: e.activation(
                out=s_lnv, in_=s_var, func=AF.Ln), waits=[(s_stat, c_var)])
            P.op("scalar", lambda e, s_lnv=s_lnv, s_rstd=s_rstd: e.activation(
                out=s_rstd, in_=s_lnv, func=AF.Exp, scale=-0.5), inc=s_stat)
            c_rstd = s_stat.n
            P.op("vector", lambda e, bi=bi, s_negm=s_negm, s_rstd=s_rstd: e.tensor_scalar(
                out=xb[bi], in0=xb[bi], scalar1=s_negm, scalar2=s_rstd,
                op0=ALU.add, op1=ALU.mult),
                 waits=[(s_scal, c_negm), (s_stat, c_rstd)], inc=s_ln)
            ln_done[t] = s_ln.n

            for g in range(8):
                bank = g % 2
                gg = t * 8 + g
                w = [(s_ln, ln_done[t]), (s_const, c_const)]
                if gg >= 2:
                    w += [(s_xn, xn_evac[gg - 2]), (s_sc, sc_evac[gg - 2])]
                for j in range(4):
                    ch = 4 * g + j
                    P.op("tensor", lambda e, bi=bi, ch=ch, j=j, bank=bank: e.transpose(
                        pb(bank)[:, j * 128:(j + 1) * 128],
                        xb[bi][:, ch * 128:(ch + 1) * 128], idf_sb),
                         waits=(w if j == 0 else []), inc=(s_tp if j == 3 else None))
                ctp = s_tp.n
                if g == 7:
                    tp_done[t] = ctp
                P.op("vector", lambda e, bank=bank, g=g, t=t: e.tensor_copy(
                    xn3[:, 4 * g:4 * g + 4, t * 128:(t + 1) * 128],
                    pb(bank).rearrange("p (c s) -> p c s", s=128)),
                     waits=[(s_tp, ctp)], inc=s_xn)
                xn_evac.append(s_xn.n)
                P.op("scalar", lambda e, bank=bank, g=g: e.copy(
                    out=scr[:, g * 512:(g + 1) * 512], in_=pb(bank)),
                     waits=[(s_xn, s_xn.n)], inc=s_sc)
                sc_evac.append(s_sc.n)
            xn_done[t] = s_xn.n
            sc_done[t] = s_sc.n

            gbank = 6 + (t % 2)
            w0 = [(s_sc, sc_done[t]), (s_const, c_const)]
            if t >= 2:
                w0.append((s_gdve, gd_done[t - 2]))
            for ch in range(NCH):
                P.op("tensor", lambda e, ch=ch, gbank=gbank: e.matmul(
                    pb(gbank)[:, 0:4], scr[:, ch * 128:(ch + 1) * 128],
                    gwt_sb[:, ch * 4:ch * 4 + 4],
                    start=(ch == 0), stop=False, skip_group_check=True),
                     waits=(w0 if ch == 0 else []))
            P.op("tensor", lambda e, gbank=gbank: e.matmul(
                pb(gbank)[:, 0:4], onesr_f, gb_sb,
                start=False, stop=True, skip_group_check=True), inc=s_gpe)
            gpe_done[t] = s_gpe.n

            ohc = oh_all[:, 4 * t:4 * t + 4]
            P.op("vector", lambda e, gbank=gbank, s_gm=s_gm: e.reduce_max(
                s_gm, pb(gbank)[:, 0:4], axis=AX),
                 waits=[(s_gpe, gpe_done[t])], inc=s_scal)
            c_gm = s_scal.n
            P.op("vector", lambda e, gbank=gbank, ohc=ohc, s_gm=s_gm: e.tensor_scalar(
                out=ohc, in0=pb(gbank)[:, 0:4], scalar1=s_gm, scalar2=None, op0=ALU.is_equal),
                 waits=[(s_scal, c_gm)], inc=s_scal)
            cch = s_scal.n
            P.op("vector", lambda e, ohc=ohc, s_cum=s_cum: e.tensor_copy(s_cum, ohc[:, 0:1]),
                 waits=[(s_scal, cch)], inc=s_scal)
            cch = s_scal.n
            for ei in range(1, 4):
                P.op("vector", lambda e, s_cum=s_cum, s_ncum=s_ncum: e.tensor_scalar(
                    out=s_ncum, in0=s_cum, scalar1=-1.0, scalar2=1.0,
                    op0=ALU.mult, op1=ALU.add),
                     waits=[(s_scal, cch)], inc=s_scal)
                cch = s_scal.n
                inc_ = s_gdve if ei == 3 else s_scal
                P.op("vector", lambda e, ei=ei, ohc=ohc, s_ncum=s_ncum: e.tensor_mul(
                    ohc[:, ei:ei + 1], ohc[:, ei:ei + 1], s_ncum),
                     waits=[(s_scal, cch)], inc=inc_)
                if ei < 3:
                    cch = s_scal.n
                    P.op("vector", lambda e, ei=ei, ohc=ohc, s_cum=s_cum: e.tensor_max(
                        s_cum, s_cum, ohc[:, ei:ei + 1]),
                         waits=[(s_scal, cch)], inc=s_scal)
                    cch = s_scal.n
            gd_done[t] = s_gdve.n

        if "statw" in tap_dram:
            P.op("sync", lambda e: e.dma_start(out=tap_dram["statw"], in_=statw),
                 waits=[(s_ln, ln_done[1])], inc=s_sp, amt=16)
        if "oh_all" in tap_dram:
            P.op("sync", lambda e: e.dma_start(out=tap_dram["oh_all"], in_=oh_all),
                 waits=[(s_gdve, s_gdve.n)], inc=s_sp, amt=16)
        if "xn_fm" in tap_dram and build_phases < 2:
            P.op("sync", lambda e: e.dma_start(out=tap_dram["xn_fm"], in_=xnfm),
                 waits=[(s_xn, s_xn.n)], inc=s_sp, amt=16)

        if build_phases >= 2:
            wv_full = bf(WQB, 16384)
            vsb = bf(VSB, 8256)
            skipset = set(skips)
            for gi in (range(8) if "V" not in skipset else ()):
                w = [(s_tp, tp_done[TT - 1]), (s_x, s_x.n)] if gi == 0 else []
                P.op("gpsimd", lambda e, gi=gi: e.dma_start(
                    out=wv_full[:, gi * 2048:(gi + 1) * 2048].rearrange(
                        "p (c s) -> p c s", s=512),
                    in_=wv_in.rearrange("(c p) s -> p c s", p=128)[:, 4 * gi:4 * gi + 4, :]),
                     waits=w, inc=s_wq, amt=16)
            c_wv = s_wq.n
            vv_done = [0] * TT
            for t in (range(TT) if "V" not in skipset else ()):
                bank = 4 + (t % 2)
                w = [(s_wq, c_wv), (s_xn, xn_done[t]), (s_const, c_const)]
                if t >= 2:
                    w.append((s_vv, vv_done[t - 2]))
                for ch in range(NCH):
                    P.op("tensor", lambda e, ch=ch, t=t, bank=bank: e.matmul(
                        pb(bank), xn_sl(ch, t * 128, 128),
                        wv_full[:, ch * 512:(ch + 1) * 512],
                        start=(ch == 0), stop=(ch == NCH - 1), skip_group_check=True),
                         waits=(w if ch == 0 else []),
                         inc=(s_vv if ch == NCH - 1 else None))
                cvp = s_vv.n
                P.op("vector", lambda e, bank=bank, t=t: e.tensor_copy(
                    vsb[:, t * (LKV * 129):(t + 1) * (LKV * 129)].rearrange(
                        "p (kv c) -> p kv c", c=129)[:, :, 0:128],
                    pb(bank).rearrange("p (kv c) -> p kv c", c=128)),
                     waits=[(s_vv, cvp)], inc=s_vv)
                vv_done[t] = s_vv.n
            c_vall = s_vv.n

            at_sb = bf(ATSB, 2048)
            P.op("gpsimd", lambda e: e.dma_start(out=at_sb, in_=at_in[:, :]),
                 waits=[(s_vv, c_vall)], inc=s_wq, amt=16)
            c_at = s_wq.n
            lr_done = [0] * TT
            for t in (range(TT) if "LORA" not in skipset else ()):
                w = [(s_wq, c_at), (s_xn, xn_done[t]), (s_gdve, gd_done[t]),
                     (s_const, c_const)]
                if t >= 1:
                    w.append((s_lr, lr_done[t - 1]))
                for ch in range(NCH):
                    P.op("tensor", lambda e, ch=ch, t=t: e.matmul(
                        pb(0)[:, 0:64], xn_sl(ch, t * 128, 128),
                        at_sb[:, ch * 64:(ch + 1) * 64],
                        start=(ch == 0), stop=(ch == NCH - 1), skip_group_check=True),
                         waits=(w if ch == 0 else []),
                         inc=(s_lr if ch == NCH - 1 else None))
                c_l1 = s_lr.n
                P.op("tensor", lambda e, t=t: e.transpose(
                    pb(1)[0:4, 0:128], oh_all[:, 4 * t:4 * t + 4], idf_sb), inc=s_lr)
                c_ot = s_lr.n
                P.op("scalar", lambda e: e.copy(out=ohT_sb, in_=pb(1)[0:4, 0:128]),
                     waits=[(s_lr, c_ot)], inc=s_lr)
                c_ot2 = s_lr.n
                P.op("tensor", lambda e: e.matmul(
                    pb(1)[:, 128:192], ohT_sb, pp_sb,
                    start=True, stop=True, skip_group_check=True),
                     waits=[(s_lr, c_ot2)], inc=s_lr)
                c_m = s_lr.n
                P.op("scalar", lambda e: e.copy(out=msb, in_=pb(1)[:, 128:192]),
                     waits=[(s_lr, c_m)], inc=s_lr)
                c_msb = s_lr.n
                P.op("vector", lambda e: e.tensor_mul(mlst, pb(0)[:, 0:64], msb),
                     waits=[(s_lr, c_msb), (s_lr, c_l1)], inc=s_lr)
                c_ml = s_lr.n
                P.op("tensor", lambda e: e.transpose(pbbf(1)[0:64, 0:128], mlst, idb_sb),
                     waits=[(s_lr, c_ml)], inc=s_lr)
                c_mlt = s_lr.n
                P.op("vector", lambda e, t=t: e.tensor_copy(
                    loraT[0:64, t * 128:(t + 1) * 128], pbbf(1)[0:64, 0:128]),
                     waits=[(s_lr, c_mlt)], inc=s_lr)
                lr_done[t] = s_lr.n
            c_lora = s_lr.n
            if "lorat" in tap_dram:
                P.op("sync", lambda e: e.dma_start(out=tap_dram["lorat"], in_=lorat_t[:, 0:S]),
                     waits=[(s_lr, c_lora)], inc=s_sp, amt=16)

            wqb = [bf(WQB, 4096), bf(WQB + 4096, 4096)]
            qst = bf(QST, 2048)
            ksb = bf(KSB, LKV * S)
            qsqs = bf(WVB, 2048)
            qk_done = [0] * 20
            ev_done = [0] * 20
            qs_done = [0] * 20
            nm_done = [0] * 20
            for ct in (range(20) if "QK" not in skipset else ()):
                bset = 0 if ct % 2 == 0 else 4
                bi = ct % 2
                w = []
                if ct < 2:
                    w.append((s_vv, c_vall))
                else:
                    w.append((s_qk, qk_done[ct - 2]))
                P.op("gpsimd", lambda e, ct=ct, bi=bi: e.dma_start(
                    out=wqb[bi], in_=wqk_t[ct, :, :]), waits=w, inc=s_wq, amt=16)
                c_w = s_wq.n
                w0 = [(s_wq, c_w), (s_xn, xn_done[TT - 1])]
                if ct == 0:
                    w0.append((s_lr, c_lora))
                if ct >= 2:
                    w0 += [(s_ev, ev_done[ct - 2]), (s_nm, nm_done[ct - 2])]
                for ch in range(NCH):
                    for tc in range(4):
                        P.op("tensor", lambda e, ch=ch, tc=tc, bi=bi, bset=bset: e.matmul(
                            pb(bset + tc), wqb[bi][:, ch * 128:(ch + 1) * 128],
                            xn_sl(ch, tc * 512, 512),
                            start=(ch == 0), stop=(ch == NCH - 1), skip_group_check=True),
                             waits=(w0 if (ch == 0 and tc == 0) else []),
                             inc=(s_qk if (ch == NCH - 1 and tc == 3) else None))
                qk_done[ct] = s_qk.n
                if ct < 16:
                    lh = ct
                    for tc in range(4):
                        w = [(s_qk, qk_done[ct])]
                        if ct >= 2:
                            w.append((s_qs, qs_done[ct - 2]))
                        P.op("scalar", lambda e, tc=tc, bset=bset: e.activation(
                            out=qst[:, tc * 512:(tc + 1) * 512], in_=pb(bset + tc),
                            func=AF.Copy, scale=SCALEQ),
                             waits=(w if tc == 0 else []))
                        P.op("scalar", lambda e, tc=tc, bset=bset: e.activation(
                            out=qsqs[:, tc * 512:(tc + 1) * 512], in_=pb(bset + tc),
                            func=AF.Square),
                             inc=(s_ev if tc == 3 else None))
                    ev_done[ct] = s_ev.n
                    P.op("sync", lambda e, lh=lh: e.dma_start(
                        out=q_dram[lh, :, :], in_=qst),
                         waits=[(s_ev, ev_done[ct])], inc=s_qs, amt=16)
                    qs_done[ct] = s_qs.n
                    for tc in range(4):
                        P.op("tensor", lambda e, tc=tc, bset=bset: e.matmul(
                            pb(bset + tc)[0:1, 0:512], ones128_bf,
                            qsqs[:, tc * 512:(tc + 1) * 512],
                            start=True, stop=True, skip_group_check=True),
                             waits=([(s_ev, ev_done[ct])] if tc == 0 else []),
                             inc=(s_nm if tc == 3 else None))
                    c_nmm = s_nm.n
                    for tc in range(4):
                        P.op("vector", lambda e, tc=tc, bset=bset: e.reduce_max(
                            nrow[0:1, 56 + tc:57 + tc], pb(bset + tc)[0:1, 0:512], axis=AX),
                             waits=([(s_nm, c_nmm)] if tc == 0 else []))
                    P.op("vector", lambda e, lh=lh: e.reduce_max(
                        nrow[0:1, lh:lh + 1], nrow[0:1, 56:60], axis=AX), inc=s_nm)
                    nm_done[ct] = s_nm.n
                else:
                    kv = ct - 16
                    for tc in range(4):
                        P.op("scalar", lambda e, tc=tc, kv=kv, bset=bset: e.activation(
                            out=ksb[:, kv * S + tc * 512:kv * S + (tc + 1) * 512],
                            in_=pb(bset + tc), func=AF.Copy),
                             waits=([(s_qk, qk_done[ct])] if tc == 0 else []),
                             inc=(s_ev if tc == 3 else None))
                    ev_done[ct] = s_ev.n
                    nm_done[ct] = s_nm.n
            c_kev = s_ev.n

            w_k0 = [(s_ev, c_kev), (s_nm, s_nm.n), (s_qk, qk_done[19])]
            kred_prev = 0
            kmm_prev = 0
            for kv in (range(LKV) if "NORM" not in skipset else ()):
                for half in range(2):
                    w_sq = list(w_k0) if (kv == 0 and half == 0) else []
                    if kv > 0 and half == 0:
                        w_sq.append((s_nm, kmm_prev))
                    P.op("scalar", lambda e, kv=kv, half=half: e.activation(
                        out=qsqs[:, half * 1024:(half + 1) * 1024],
                        in_=ksb[:, kv * S + half * 1024:kv * S + (half + 1) * 1024],
                        func=AF.Square),
                         waits=w_sq, inc=(s_nm if half == 1 else None))
                c_ks = s_nm.n
                if "NORMB" in skipset:
                    continue
                for tc in range(4):
                    P.op("tensor", lambda e, tc=tc: e.matmul(
                        pb(4 + tc)[0:1, 0:512], ones128_bf, qsqs[:, tc * 512:(tc + 1) * 512],
                        start=True, stop=True, skip_group_check=True),
                         waits=([(s_nm, c_ks), (s_qk, qk_done[19]), (s_ev, c_kev),
                                 (s_nm, kred_prev)] if tc == 0 else []),
                         inc=(s_nm if tc == 3 else None))
                kmm_prev = s_nm.n
                c_kmm = s_nm.n
                if "NORMC" in skipset:
                    continue
                cred = 0
                for tc in range(4):
                    P.op("vector", lambda e, tc=tc: e.reduce_max(
                        nrow[0:1, 56 + tc:57 + tc], pb(4 + tc)[0:1, 0:512], axis=AX),
                         waits=([(s_nm, c_kmm)] if tc == 0 else [(s_scal, cred)]),
                         inc=s_scal)
                    cred = s_scal.n
                P.op("vector", lambda e, kv=kv: e.reduce_max(
                    nrow[0:1, 16 + kv:17 + kv], nrow[0:1, 56:60], axis=AX),
                     waits=[(s_scal, cred)], inc=s_nm)
                kred_prev = s_nm.n
            c_kn = s_nm.n
            for kv in (range(LKV) if ("NORM" not in skipset and "NORMD" not in skipset
                                       and "NORMC" not in skipset and "NORMB" not in skipset)
                       else ()):
                P.op("vector", lambda e, kv=kv: e.tensor_scalar(
                    out=nrow[0:1, 24 + 4 * kv:28 + 4 * kv],
                    in0=nrow[0:1, 4 * kv:4 * kv + 4],
                    scalar1=nrow[0:1, 16 + kv:17 + kv], scalar2=None, op0=ALU.mult),
                     waits=[(s_nm, c_kn)] if kv == 0 else [], inc=(s_nm if kv == 3 else None))
            c_prod = s_nm.n
            if {"NORM", "NORMB", "NORMC", "NORMD", "NORME"} & skipset:
                c_negch = s_nm.n
            if not ({"NORM", "NORMB", "NORMC", "NORMD", "NORME"} & skipset):
                P.op("scalar", lambda e: e.activation(out=nrow[0:1, 40:56],
                                                      in_=nrow[0:1, 24:40], func=AF.Sqrt),
                     waits=[(s_nm, c_prod)])
                P.op("scalar", lambda e: e.activation(out=nrow[0:1, 24:40],
                                                      in_=nrow[0:1, 40:56], func=AF.Copy,
                                                      scale=-SCALEQ), inc=s_nm)
                c_negc = s_nm.n
                P.op("tensor", lambda e: e.matmul(pb(5)[:, 0:16], onesr_f, nrow[0:1, 24:40],
                                                  start=True, stop=True, skip_group_check=True),
                     waits=[(s_nm, c_negc), (s_vv, c_vall), (s_ev, c_kev)], inc=s_nm)
                c_bc = s_nm.n
                P.op("vector", lambda e: e.tensor_copy(negch_sb, pb(5)[:, 0:16]),
                     waits=[(s_nm, c_bc)], inc=s_nm)
                c_negch = s_nm.n

            for half in (range(2) if "SPILL" not in skipset else ()):
                P.op("sync", lambda e, half=half: e.dma_start(
                    out=xnh_dram.rearrange("(c two) p s -> two p c s", two=2)[half],
                    in_=xn3[:, :, half * 1024:(half + 1) * 1024]),
                     waits=[(s_xn, xn_done[TT - 1])], inc=s_sp, amt=16)
            c_xnsp = s_sp.n
            if "SPILL" not in skipset:
                P.op("sync", lambda e: e.dma_start(
                    out=lorat_dram.rearrange("h r c -> r h c"),
                    in_=lorat_t[0:65, 0:S].rearrange("r (h c) -> r h c", c=1024)),
                     waits=[(s_lr, c_lora)], inc=s_sp, amt=16)
            c_lrsp = s_sp.n

            if "xn_fm" in tap_dram:
                P.op("sync", lambda e: e.dma_start(out=tap_dram["xn_fm"], in_=xnfm),
                     waits=[(s_qk, s_qk.n), (s_lr, c_lora)], inc=s_sp, amt=16)
            if "k_sb" in tap_dram:
                P.op("sync", lambda e: e.dma_start(out=tap_dram["k_sb"], in_=ksb),
                     waits=[(s_ev, s_ev.n)], inc=s_sp, amt=16)
            if "v_sb" in tap_dram:
                P.op("sync", lambda e: e.dma_start(out=tap_dram["v_sb"], in_=vsb),
                     waits=[(s_vv, c_vall)], inc=s_sp, amt=16)
            if "negch" in tap_dram:
                P.op("sync", lambda e: e.dma_start(out=tap_dram["negch"], in_=negch_sb),
                     waits=[(s_nm, c_negch)], inc=s_sp, amt=16)
            if "q_dram" in tap_dram or True:
                pass

        if build_phases >= 3:
            dm = f32v(DM, MW)
            dmneg = f32v(DMNEG, MW)
            mbf = bf(MBF, MW)
            P.op("gpsimd", lambda e: e.iota(dm, [[1, MW]], base=-384, channel_multiplier=-1,
                                            allow_small_or_imprecise_dtypes=True),
                 waits=[(s_qk, qk_done[19]), (s_lr, c_lora), (s_sp, c_xnsp)], inc=s_dm)
            c_i = s_dm.n
            P.op("vector", lambda e: e.tensor_scalar(out=dmneg, in0=dm, scalar1=0.0,
                                                     scalar2=None, op0=ALU.is_lt),
                 waits=[(s_dm, c_i)], inc=s_dm)
            P.op("vector", lambda e: e.tensor_scalar(out=dm, in0=dm, scalar1=0.0,
                                                     scalar2=None, op0=ALU.max), inc=s_dm)
            c_relu = s_dm.n
            P.op("scalar", lambda e: e.activation(out=dm, in_=dm, func=AF.Ln),
                 waits=[(s_dm, c_relu)])
            P.op("scalar", lambda e: e.activation(out=dm, in_=dm, func=AF.Exp, scale=0.5),
                 inc=s_dm)
            c_sq = s_dm.n
            P.op("vector", lambda e: e.scalar_tensor_tensor(
                out=dm, in0=dmneg, scalar=BIGM, in1=dm, op0=ALU.mult, op1=ALU.add),
                 waits=[(s_dm, c_sq)], inc=s_dm)
            c_dm = s_dm.n
            P.op("vector", lambda e: e.tensor_copy(mbf, dm), waits=[(s_dm, c_dm)],
                 inc=s_dm)
            c_mbf = s_dm.n

            ctxs = bf(CTX, LH * S)
            ctx3 = ctxs.rearrange("p (h s) -> p h s", s=S)
            qh = [bf(QH, 2048), bf(QH + 2048, 2048)]
            pbuf = [bf(PBUF, 512), bf(PBUF + 512, 512), bf(PBUF + 1024, 512)]
            stg = bf(STG, 2048)
            scid = [bf(SCID, 128), bf(SCID + 128, 128)]
            vsb_a = bf(VSB, 8256)
            LA = 2
            iters = []
            for lh in range(LH):
                for qc in range(QC):
                    nkb = 4 * (qc + 1)
                    for kb in range(nkb):
                        iters.append((lh, qc, kb, nkb))
            NIT = len(iters)
            sS = [0, 0, 0]
            pfree = [0, 0, 0]
            qfree = [0, 0]
            scfree = [0, 0]
            c_s = [0] * NIT
            c_p = [0] * NIT
            c_avv = [0] * NIT
            ql_done = [0] * LH
            c_scid = [0] * LH
            att_ctx_free = 0
            ctx_ev = 0
            trp_prev = 0
            norm_pend = []
            c_att_last_sA = 0

            def emit_norm(lh, qc, c_avd):
                for sub in range(4):
                    rc = statw[:, 32 + sub:33 + sub]
                    P.op("vector", lambda e, rc=rc, sub=sub: e.reciprocal(
                        rc, pb(3 + sub)[:, 128:129]),
                         waits=([(s_av, c_avd), (s_tp, trp_prev)] if sub == 0 else []),
                         inc=s_scal)
                    c_r = s_scal.n
                    P.op("vector", lambda e, rc=rc, sub=sub: e.tensor_scalar(
                        out=stg[:, sub * 128:(sub + 1) * 128], in0=pb(3 + sub)[:, 0:128],
                        scalar1=rc, scalar2=None, op0=ALU.mult),
                         waits=[(s_scal, c_r)], inc=s_cx)
                return s_cx.n

            def emit_norm_pe(lh, qc, c_nrm):
                for sub in range(4):
                    P.op("tensor", lambda e, sub=sub: e.transpose(
                        pbbf(7)[:, sub * 128:(sub + 1) * 128],
                        stg[:, sub * 128:(sub + 1) * 128], idb_sb),
                         waits=([(s_cx, c_nrm), (s_cx, ctx_ev)] if sub == 0 else []),
                         inc=(s_tp if sub == 3 else None))
                c_trp = s_tp.n
                P.op("vector", lambda e, lh=lh, qc=qc: e.tensor_copy(
                    ctxs[:, lh * S + qc * 512:lh * S + (qc + 1) * 512], pbbf(7)[:, 0:512]),
                     waits=[(s_tp, c_trp)], inc=s_cx)
                return c_trp, s_cx.n

            for pos in range(NIT + LA + 1):
                if pos < NIT:
                    it = pos
                    lh, qc, kb, nkb = iters[it]
                    kv = lh // 4
                    j = it % 3
                    if qc == 0 and kb == 0:
                        bi = lh % 2
                        w = [(s_qs, qs_done[lh]), (s_sp, c_xnsp), (s_sp, c_lrsp),
                             (s_qk, qk_done[19]), (s_lr, c_lora)]
                        if lh >= 2:
                            w.append((s_sA, qfree[bi]))
                        P.op("sync", lambda e, lh=lh, bi=bi: e.dma_start(
                            out=qh[bi], in_=q_dram[lh, :, :]), waits=w, inc=s_ql, amt=16)
                        ql_done[lh] = s_ql.n
                        w = [(s_const, c_const)]
                        if lh >= 2:
                            w.append((s_sA, scfree[bi]))
                        P.op("vector", lambda e, lh=lh, bi=bi: e.tensor_scalar(
                            out=scid[bi], in0=idb_sb, scalar1=nsl_sb[:, lh:lh + 1],
                            scalar2=None, op0=ALU.mult), waits=w, inc=s_scal)
                        c_scid[lh] = s_scal.n
                    off = 384 + qc * 512 - kb * 128
                    w = [(s_ql, ql_done[lh]), (s_scal, c_scid[lh]), (s_dm, c_mbf)]
                    if it >= 3:
                        w.append((s_sP, sS[j]))
                    if it == 0:
                        w += [(s_ev, c_kev), (s_nm, c_kn)]
                    P.op("tensor", lambda e, off=off, j=j, bi=lh % 2: e.matmul(
                        pb(j), scid[bi], mbf[:, off:off + 512],
                        start=True, stop=False, skip_group_check=True), waits=w)
                    P.op("tensor", lambda e, kv=kv, kb=kb, qc=qc, j=j, bi=lh % 2: e.matmul(
                        pb(j), ksb[:, kv * S + kb * 128:kv * S + kb * 128 + 128],
                        qh[bi][:, qc * 512:(qc + 1) * 512],
                        start=False, stop=True, skip_group_check=True), inc=s_sA)
                    c_s[it] = s_sA.n
                    if qc == QC - 1 and kb == nkb - 1:
                        qfree[lh % 2] = c_s[it]
                        scfree[lh % 2] = c_s[it]
                        c_att_last_sA = c_s[it]
                    w = [(s_sA, c_s[it]), (s_nm, c_negch)]
                    if it >= 3:
                        w.append((s_av, pfree[it % 3]))
                    P.op("scalar", lambda e, j=j, it=it, lh=lh: e.activation(
                        out=pbuf[it % 3], in_=pb(j), func=AF.Exp,
                        bias=negch_sb[:, lh:lh + 1]), waits=w, inc=s_sP)
                    c_p[it] = s_sP.n
                    sS[j] = c_p[it]
                ai = pos - LA
                if 0 <= ai < NIT:
                    lh, qc, kb, nkb = iters[ai]
                    kv = lh // 4
                    for sub in range(4):
                        w = []
                        if sub == 0:
                            w.append((s_sP, c_p[ai]))
                            if kb == 0:
                                w.append((s_cx, att_ctx_free))
                        P.op("tensor", lambda e, sub=sub, kb=kb, ai=ai, kv=kv, nkb=nkb: e.matmul(
                            pb(3 + sub)[:, 0:129], pbuf[ai % 3][:, sub * 128:(sub + 1) * 128],
                            vsb_a[:, kb * (LKV * 129) + kv * 129:
                                  kb * (LKV * 129) + kv * 129 + 129],
                            start=(kb == 0), stop=(kb == nkb - 1), skip_group_check=True),
                             waits=w, inc=(s_av if sub == 3 else None))
                    c_avv[ai] = s_av.n
                    pfree[ai % 3] = c_avv[ai]
                    if kb == nkb - 1:
                        c_nrm = emit_norm(lh, qc, c_avv[ai])
                        att_ctx_free = c_nrm
                        norm_pend.append((lh, qc, c_nrm, pos + 2))
                while norm_pend and norm_pend[0][3] <= pos:
                    nlh, nqc, c_nrm, _ = norm_pend.pop(0)
                    trp_prev, ctx_ev = emit_norm_pe(nlh, nqc, c_nrm)
            while norm_pend:
                nlh, nqc, c_nrm, _ = norm_pend.pop(0)
                trp_prev, ctx_ev = emit_norm_pe(nlh, nqc, c_nrm)
            c_att = s_cx.n
            if "ctx_sb" in tap_dram:
                P.op("sync", lambda e: e.dma_start(out=tap_dram["ctx_sb"], in_=ctxs),
                     waits=[(s_cx, c_att)], inc=s_sp, amt=16)

        if build_phases >= 4 and "EXCH" not in set(skips):
            for half in range(2):
                P.op("sync", lambda e, half=half: e.dma_start(
                    out=cc_in[half * LH * 128:(half + 1) * LH * 128, :].rearrange(
                        "(h p) c -> p h c", p=128),
                    in_=ctx3[:, :, half * 1024:(half + 1) * 1024]),
                     waits=[(s_cx, c_att)], inc=s_cc, amt=16)
                P.op("sync", lambda e, half=half: e.dma_start(
                    out=ctxh_dram.rearrange("(h two) p s -> two p h s", two=2)[half],
                    in_=ctx3[:, :, half * 1024:(half + 1) * 1024]),
                     waits=([(s_cx, c_att)] if half == 0 else []), inc=s_cc, amt=16)
            c_send = s_cc.n
            P.op("gpsimd", lambda e: e.collective_compute(
                "AllGather", ALU.bypass, replica_groups=[list(range(8))],
                ins=[cc_in], outs=[cc_out.rearrange("b h r c -> (b h r) c")]),
                 waits=[(s_cc, c_send)], inc=s_cc, amt=1)
            c_ag = s_cc.n
            ctx_res = bf(CTXRES, 32768)
            P.op("sync", lambda e: e.dma_start(
                out=loraT_own[0:65, :],
                in_=lorat_dram[bass.ts(sym["own_half"], 1), :, :].rearrange(
                    "a r c -> (a r) c")),
                 waits=[(s_sp, c_lrsp), (s_sym, 1), (s_cc, c_ag)], inc=s_cc, amt=16)
            c_lown = s_cc.n
            ctxh4 = ctxh_dram.rearrange("(h two) p s -> h two p s", two=2)
            P.op("scalar", lambda e: e.dma_start(
                out=ctx_res[:, 0:16384].rearrange("p (h s) -> p h s", s=1024),
                in_=ctxh4[:, bass.ts(sym_sc["own_half"], 1), :, :].rearrange(
                    "h a p s -> p h (a s)")),
                 waits=[(s_cc, c_ag), (s_sym, 1), (s_sA, c_att_last_sA),
                        (s_av, s_av.n)], inc=s_cc, amt=16)
            P.op("scalar", lambda e: e.dma_start(
                out=ctx_res[:, 16384:32768].rearrange("p (h s) -> p h s", s=1024),
                in_=cc_out[bass.ts(sym_sc["pblk"], 1), :, :, :].rearrange(
                    "a h p c -> p (a h) c")),
                 waits=[(s_cc, c_ag), (s_sA, c_att_last_sA),
                        (s_av, s_av.n)], inc=s_cc, amt=16)
            c_asm = s_cc.n

        if build_phases >= 5:
            xn_res = bf(XRES, 32768)
            ring = [bf(RING + r * 8192, 8192) for r in range(3)]
            xsl = [f32v(MXS, 512), f32v(MXS + 1024, 512)]
            outs = [f32v(MOU, 512), f32v(MOU + 1024, 512)]
            bbt_sb = bf(MBB, 4096)
            bbt_w = [(s_sp, c_xnsp), (s_qk, qk_done[19]), (s_lr, c_lora)]
            if "EXCH" not in skipset:
                bbt_w.append((s_cc, c_asm))
            P.op("gpsimd", lambda e: e.dma_start(out=bbt_sb[0:65, :], in_=bbt[:, :]),
                 waits=bbt_w, inc=s_mw, amt=16)
            c_bbt = s_mw.n
            xnh4 = xnh_dram.rearrange("(c two) p s -> c two p s", two=2)
            xres_w = [(s_sym, 1)]
            if "EXCH" not in skipset:
                xres_w.append((s_cc, c_asm))
            else:
                xres_w += [(s_cx, c_att)]
            P.op("sync", lambda e: e.dma_start(
                out=xn_res.rearrange("p (c s) -> p c s", s=1024),
                in_=xnh4[:, bass.ts(sym["own_half"], 1), :, :].rearrange(
                    "c a p s -> p c (a s)")),
                 waits=xres_w, inc=s_ml, amt=16)
            c_xres = s_ml.n
            if "xnres" in tap_dram:
                P.op("sync", lambda e: e.dma_start(out=tap_dram["xnres"], in_=xn_res),
                     waits=[(s_ml, c_xres)], inc=s_sp, amt=16)

            def w_src(q, ocb, hs):
                base = wWt if q == 0 else wop
                return base.rearrange("(c p) s -> p c s", p=128)[
                    :, hs * 16:(hs + 1) * 16, ocb * 512:(ocb + 1) * 512]

            def pe_fence(waits):
                for i in range(64):
                    P.op("tensor", lambda e: e.matmul(
                        pb(7)[0:1, 0:1], ones128_bf[0:1, 0:1], ones128_bf[0:1, 0:1],
                        start=True, stop=True, skip_group_check=True),
                         waits=(waits if i == 0 else []))

            ring_free = {}
            mp_ocb = {}
            me_grp = {}
            out_done = {}
            gran_ready = {}
            c_passw_out = 0
            for pq in (range(2) if "WO" not in skipset else range(1)):
                if pq == 0:
                    fw = [(s_ml, c_xres), (s_mw, c_bbt), (s_cx, c_att)]
                    if "EXCH" not in skipset:
                        fw.append((s_cc, c_asm))
                    pe_fence(fw)
                else:
                    pe_fence([(s_cc, c_asm)])
                for ocb in range(8):
                    g0 = pq * 16 + ocb * 2
                    for hs in range(2):
                        g = g0 + hs
                        slot = g % 3
                        w = []
                        if g < 3:
                            w += [(s_cx, c_att), (s_sA, c_att_last_sA), (s_av, s_av.n)]
                            if "EXCH" not in skipset:
                                w.append((s_cc, c_asm))
                        prev = ring_free.get(slot)
                        if prev is not None:
                            w.append((s_mp, prev))
                        P.op("gpsimd", lambda e, pq=pq, ocb=ocb, hs=hs, slot=slot: e.dma_start(
                            out=ring[slot].rearrange("p (c s) -> p c s", s=512),
                            in_=w_src(pq, ocb, hs)), waits=w, inc=s_mw, amt=16)
                        gran_ready[g] = s_mw.n
                        if g == 0 and "ring0" in tap_dram:
                            P.op("sync", lambda e: e.dma_start(out=tap_dram["ring0"],
                                                               in_=ring[0]),
                                 waits=[(s_mw, gran_ready[0])], inc=s_mw, amt=16)
                            gran_ready[0] = s_mw.n
                    for grp in range(2):
                        bset = grp * 4
                        for c in range(NCH):
                            hs = c // 16
                            slot = (g0 + hs) % 3
                            for ti in range(4):
                                t = grp * 4 + ti
                                w = []
                                if c == 0 and ti == 0:
                                    w = [(s_mw, gran_ready[g0]), (s_mw, c_bbt)]
                                    if pq == 0:
                                        w.append((s_ml, c_xres))
                                    else:
                                        w.append((s_cc, c_asm))
                                    prevg = me_grp.get((pq * 8 + ocb - 1, grp))
                                    if prevg:
                                        w.append((s_me, prevg))
                                elif c == 16 and ti == 0:
                                    w = [(s_mw, gran_ready[g0 + 1])]
                                lhs_res = xn_res if pq == 0 else ctx_res
                                stop_f = ((pq == 1 or "NOLORA" in skipset) and c == NCH - 1)
                                P.op("tensor", lambda e, c=c, t=t, slot=slot, bset=bset,
                                     ti=ti, lhs_res=lhs_res, stop_f=stop_f: e.matmul(
                                        pb(bset + ti),
                                        lhs_res[:, c * 1024 + t * 128:c * 1024 + t * 128 + 128],
                                        ring[slot][:, (c % 16) * 512:(c % 16 + 1) * 512],
                                        start=(c == 0), stop=stop_f, skip_group_check=True),
                                     waits=w,
                                     inc=(s_mp if stop_f else None))
                        for ti in range(4):
                            t = grp * 4 + ti
                            if pq == 0 and "NOLORA" not in skipset:
                                P.op("tensor", lambda e, t=t, ocb=ocb, bset=bset, ti=ti: e.matmul(
                                    pb(bset + ti), loraT_own[0:65, t * 128:(t + 1) * 128],
                                    bbt_sb[0:65, ocb * 512:(ocb + 1) * 512],
                                    start=False, stop=True, skip_group_check=True),
                                     waits=([(s_cc, c_lown)] if (ocb == 0 and ti == 0) else []),
                                     inc=s_mp)
                                c_mmt = s_mp.n
                            else:
                                c_mmt = s_mp.n - (3 - ti)
                            gi8 = (pq * 8 + ocb) * 8 + t
                            sl = gi8 % 2
                            if pq == 0:
                                w = [(s_me, out_done.get(("ev", gi8 - 2), 0))]
                                P.op("sync", lambda e, t=t, ocb=ocb, sl=sl: e.dma_start(
                                    out=xsl[sl],
                                    in_=x_own[t * 128:(t + 1) * 128, ocb * 512:(ocb + 1) * 512]),
                                     waits=[(s, v) for s, v in w if v], inc=s_ml, amt=16)
                                c_x = s_ml.n
                                P.op("vector", lambda e, sl=sl, bset=bset, ti=ti: e.tensor_add(
                                    outs[sl], pb(bset + ti), xsl[sl]),
                                     waits=[(s_mp, c_mmt), (s_ml, c_x),
                                            (s_out, out_done.get(("dma", gi8 - 2), 0))],
                                     inc=s_me)
                            else:
                                P.op("vector", lambda e, sl=sl, bset=bset, ti=ti: e.tensor_copy(
                                    outs[sl], pb(bset + ti)),
                                     waits=[(s_mp, c_mmt),
                                            (s_out, out_done.get(("dma", gi8 - 2), 0))],
                                     inc=s_me)
                            out_done[("ev", gi8)] = s_me.n
                            if pq == 0:
                                P.op("sync", lambda e, t=t, ocb=ocb, sl=sl: e.dma_start(
                                    out=y[t * 128:(t + 1) * 128, ocb * 512:(ocb + 1) * 512],
                                    in_=outs[sl]),
                                     waits=[(s_me, out_done[("ev", gi8)])],
                                     inc=s_out, amt=16)
                            else:
                                P.op("gpsimd", lambda e, t=t, ocb=ocb, sl=sl: e.dma_start(
                                    out=y[t * 128:(t + 1) * 128, ocb * 512:(ocb + 1) * 512],
                                    in_=outs[sl], accum_op=ALU.add),
                                     waits=[(s_me, out_done[("ev", gi8)]),
                                            (s_out, c_passw_out)],
                                     inc=s_out, amt=16)
                            out_done[("dma", gi8)] = s_out.n
                        me_grp[(pq * 8 + ocb, grp)] = s_me.n
                    mp_ocb[pq * 8 + ocb] = s_mp.n
                    for hs in range(2):
                        ring_free[(g0 + hs) % 3] = mp_ocb[pq * 8 + ocb]
                if pq == 0:
                    c_passw_out = s_out.n
            if "xnres2" in tap_dram:
                P.op("sync", lambda e: e.dma_start(out=tap_dram["xnres2"], in_=xn_res),
                     waits=[(s_mp, mp_ocb[7])], inc=s_sp, amt=16)
            P.op("sync", lambda e: e.nop(), waits=[(s_out, s_out.n)])

        with nc.Block() as block:
            @block.tensor
            def _(e):
                for fn in P.ops["tensor"]:
                    fn(e)

            @block.vector
            def _(e):
                for fn in P.ops["vector"]:
                    fn(e)

            @block.scalar
            def _(e):
                for fn in P.ops["scalar"]:
                    fn(e)

            @block.gpsimd
            def _(e):
                for fn in P.ops["gpsimd"]:
                    fn(e)

            @block.sync
            def _(e):
                for fn in P.ops["sync"]:
                    fn(e)

    return nc


def _prep_inputs(x, wqkv, wo, W, b, A, Bm, gw, gb):
    slopes = _alibi_slopes()
    wWt = np.ascontiguousarray(W.T)
    at_t = np.ascontiguousarray(
        A.T.reshape(32, 128, 64).transpose(1, 0, 2).reshape(128, 2048))
    bbt = np.ascontiguousarray(np.concatenate([Bm.T, b[None, :]], axis=0))
    gwt_t = np.ascontiguousarray(
        gw.T.reshape(32, 128, 4).transpose(1, 0, 2).reshape(128, 128))
    gb2 = np.ascontiguousarray(gb.reshape(1, 4))
    maps = []
    for c in range(8):
        bq, p = c // 2, c % 2
        qcols = wqkv[:, 2048 * p:2048 * p + 2048]
        kcols = wqkv[:, 4096 + 512 * p:4096 + 512 * p + 512]
        cols20 = np.concatenate([qcols, kcols], axis=1)
        wqk = np.ascontiguousarray(
            cols20.reshape(32, 128, 20, 128).transpose(2, 1, 0, 3).reshape(20, 128, 4096))
        wv = np.ascontiguousarray(wqkv[:, 5120 + 512 * p:5120 + 512 * p + 512])
        perm = np.r_[2048 * p:2048 * p + 2048, 2048 * (1 - p):2048 * (1 - p) + 2048]
        wopm = np.ascontiguousarray(wo[perm])
        nsl = np.ascontiguousarray(
            np.broadcast_to(-slopes[16 * p:16 * p + 16][None, :], (128, LH)))
        prm = np.array([[8 * p, 8 * (1 - p), c ^ 1, p, 1 - p, (c ^ 1) * 2 + p, 0, 0]],
                       dtype=np.int32)
        maps.append({
            "x_seq": np.ascontiguousarray(x[bq]),
            "x_own": np.ascontiguousarray(x[bq, 1024 * p:1024 * p + 1024]),
            "wqk_t": wqk, "wv_in": wv, "wWt": wWt, "wop": wopm,
            "at_in": at_t, "bbt": bbt, "gwt_in": gwt_t, "gb_in": gb2,
            "nsl_in": nsl, "prm": prm,
        })
    return maps


_NC_CACHE = {}


def kernel(L0_attn_hidden_states, L0_attn_w_qkv, L0_attn_w_o, L0_ffn_W, L0_ffn_b,
           L0_ffn_A, L0_ffn_B, L0_ffn_gate_w, L0_ffn_gate_b):
    x = np.ascontiguousarray(np.asarray(L0_attn_hidden_states, dtype=np.float32))
    wqkv = np.asarray(L0_attn_w_qkv, np.float32)
    wo = np.asarray(L0_attn_w_o, np.float32)
    W = np.asarray(L0_ffn_W, np.float32)
    b = np.asarray(L0_ffn_b, np.float32)
    A = np.asarray(L0_ffn_A, np.float32)
    Bm = np.asarray(L0_ffn_B, np.float32)
    gw = np.asarray(L0_ffn_gate_w, np.float32)
    gb = np.asarray(L0_ffn_gate_b, np.float32)

    key = (tuple(_TAPS), _BUILD_PHASES, tuple(_SKIPS))
    if key not in _NC_CACHE:
        _NC_CACHE.clear()
        _NC_CACHE[key] = build(taps=_TAPS, build_phases=_BUILD_PHASES, skips=_SKIPS)
    nc = _NC_CACHE[key]

    in_maps = _prep_inputs(x, wqkv, wo, W, b, A, Bm, gw, gb)
    res = run_bass_kernel_spmd(nc, in_maps, core_ids=list(range(8)))
    out = np.empty((B, S, D), dtype=np.float32)
    for c in range(8):
        bq, p = c // 2, c % 2
        out[bq, 1024 * p:1024 * p + 1024] = res.results[c]["y"]
    kernel._last_results = res.results
    return out
